# version 05
# baseline (speedup 1.0000x reference)
import numpy as np
import ml_dtypes
import concourse.bass as bass
import concourse.mybir as mybir
from concourse.bass_utils import run_bass_kernel_spmd

F32 = mybir.dt.float32
BF16 = mybir.dt.bfloat16
ALU = mybir.AluOpType
AF = mybir.ActivationFunctionType
AX = mybir.AxisListType

T = 2048
D = 1024
NF = 8
NH = 16
HD = 64
CH = 128
QT = 512
NQ = T // QT
CPQ = QT // CH
HT = 1024
NS = 512
E = 12
HS = 2048
HR = 3072
TAU = 1.5
EPS_LN = 1e-5
EPS_AT = 1e-12

BC_EPS = 0
BC_BS1 = 1
BC_BS2 = 33
BC_BE1 = 41
NBCOL = 41 + E * 48

PHASE_LIMIT = 2
ATTN_STAGE = 3
NQL = NQ
NKT = 32


class _Eng:
    def __init__(self, name, step):
        self.name = name
        self.step = step
        self.ops = []
        self.sem = None


class _Op:
    __slots__ = ("eng", "fn", "inc", "value")

    def __init__(self, eng, fn):
        self.eng = eng
        self.fn = fn
        self.inc = False
        self.value = None


class Prog:
    def __init__(self):
        self.pe = _Eng("pe", 1)
        self.ve = _Eng("ve", 1)
        self.ac = _Eng("ac", 1)
        self.gp = _Eng("gp", 1)
        self.sy = _Eng("sy", 1)
        self.wds = [_Eng(f"wd{i}", 16) for i in range(12)]
        self.xds = [_Eng(f"xds{i}", 16) for i in range(2)]
        self.wdx = _Eng("wdx", 16)
        self.xd = _Eng("xd", 16)
        self.st = _Eng("st", 16)
        self.issue = [self.pe, self.ve, self.ac, self.gp, self.sy]
        self.last_dma = {}

    def op(self, eng, fn, deps=()):
        for d in deps:
            self.wait(eng, d)
        o = _Op(eng, fn)
        eng.ops.append(o)
        return o

    def dma(self, issue_eng, done_eng, fn, deps=()):
        for d in deps:
            self.wait(issue_eng, d)
        o = _Op(done_eng, fn)
        o.inc = True
        issue_eng.ops.append(o)
        self.last_dma[id(done_eng)] = o
        return o

    def wait(self, eng, opref):
        if opref is None:
            return
        if opref.eng in (self.xd, self.st, self.wdx):
            opref = self.last_dma[id(opref.eng)]
        opref.inc = True
        eng.ops.append(("w", opref))

    def barrier(self, engines):
        lasts = []
        for e in engines:
            for it in reversed(e.ops):
                if isinstance(it, _Op):
                    lasts.append(it)
                    break
        for e in engines:
            for l in lasts:
                self.wait(e, l)

    def finalize(self):
        counts = {}
        for e in self.issue:
            for it in e.ops:
                if isinstance(it, _Op) and it.inc:
                    k = id(it.eng)
                    counts[k] = counts.get(k, 0) + 1
                    it.value = counts[k] * it.eng.step

    def replay(self, e, iface):
        last_wait = {}
        for it in e.ops:
            if isinstance(it, _Op):
                instr = it.fn(iface)
                if it.inc:
                    instr.then_inc(it.eng.sem, it.eng.step)
            else:
                _, ref = it
                assert ref.value is not None
                k = id(ref.eng)
                if last_wait.get(k, -1) >= ref.value:
                    continue
                iface.wait_ge(ref.eng.sem, ref.value)
                last_wait[k] = ref.value


def build_program():
    nc = bass.Bass("TRN2", target_bir_lowering=False)
    p = Prog()

    xT = nc.dram_tensor("xT", [D, T], F32, kind="ExternalInput").ap()
    wqT = nc.dram_tensor("wqT", [D, D], F32, kind="ExternalInput").ap()
    wkT = nc.dram_tensor("wkT", [D, D], F32, kind="ExternalInput").ap()
    wvT = nc.dram_tensor("wvT", [D, D], F32, kind="ExternalInput").ap()
    woT = nc.dram_tensor("woT", [D, D], F32, kind="ExternalInput").ap()
    ws1 = nc.dram_tensor("ws1", [D, 2 * HS], F32, kind="ExternalInput").ap()
    ws2 = nc.dram_tensor("ws2", [HS, D], F32, kind="ExternalInput").ap()
    we1 = nc.dram_tensor("we1", [E, D, 2 * HR], F32, kind="ExternalInput").ap()
    we2 = nc.dram_tensor("we2", [E, HR, D], F32, kind="ExternalInput").ap()
    wr = nc.dram_tensor("wr", [D, E], F32, kind="ExternalInput").ap()
    be2d = nc.dram_tensor("be2", [E, D], F32, kind="ExternalInput").ap()
    bias_in = nc.dram_tensor("bias", [128, NBCOL], F32, kind="ExternalInput").ap()
    brb_in = nc.dram_tensor("brb", [128, E], F32, kind="ExternalInput").ap()
    maskT_in = nc.dram_tensor("maskT", [CH, CH], BF16, kind="ExternalInput").ap()
    bdiag_in = nc.dram_tensor("bdiag", [CH, CH], BF16, kind="ExternalInput").ap()
    idb_in = nc.dram_tensor("idb", [128, 128], BF16, kind="ExternalInput").ap()
    idf_in = nc.dram_tensor("idf", [128, 128], F32, kind="ExternalInput").ap()
    onesT_in = nc.dram_tensor("onesT", [128, 128], F32, kind="ExternalInput").ap()
    sel0_in = nc.dram_tensor("sel0", [128, 128], F32, kind="ExternalInput").ap()
    sel_in = nc.dram_tensor("sel", [128, E * 128], F32, kind="ExternalInput").ap()
    oneb_in = nc.dram_tensor("onebf", [128, 1], BF16, kind="ExternalInput").ap()
    out = nc.dram_tensor("out", [D, T], F32, kind="ExternalOutput").ap()
    x1s = nc.dram_tensor("x1s", [D, T], F32, kind="Internal").ap()

    we1f = we1.rearrange("e r m -> (e r) m")
    we2f = we2.rearrange("e r d -> (e r) d")

    ctx = []

    def sb(name, shape, dtype):
        cm = nc.sbuf_tensor(name, shape, dtype)
        h = cm.__enter__()
        ctx.append(cm)
        return h

    def psb(name, shape, dtype=F32):
        cm = nc.psum_tensor(name, shape, dtype)
        h = cm.__enter__()
        ctx.append(cm)
        return h

    class scope:
        def __enter__(self):
            self.mark = len(ctx)
            return self

        def __exit__(self, *a):
            while len(ctx) > self.mark:
                ctx.pop().__exit__(None, None, None)
            return False

    c_mask = sb("c_mask", [CH, CH], BF16)
    c_bdiag = sb("c_bdiag", [CH, CH], BF16)
    c_idb = sb("c_idb", [128, 128], BF16)
    c_idf = sb("c_idf", [128, 128], F32)
    c_onesT = sb("c_onesT", [128, 128], F32)
    c_oneb = sb("c_oneb", [128, 1], BF16)
    c_sel0 = sb("c_sel0", [128, 128], F32)
    c_sel = sb("c_sel", [128, E * 128], F32)
    c_rows = sb("c_rows", [128, 3 * NS], F32)
    c_bias = sb("c_bias", [128, NBCOL], F32)
    c_brb = sb("c_brb", [128, E], F32)
    c_be2s = sb("c_be2s", [128, D], F32)
    c_wrb = sb("c_wrb", [128, NF * E], BF16)
    wring = sb("wring", [128, 12 * 1024], BF16)
    g_psA = psb("g_psA", [128, NS])
    g_psB = psb("g_psB", [128, NS])
    g_psC = psb("g_psC", [128, NS])
    g_psD = psb("g_psD", [128, NS])
    g_psE = psb("g_psE", [128, NS])
    g_psF = psb("g_psF", [128, NS])
    g_psG = psb("g_psG", [128, NS])
    g_psT = psb("g_psT", [128, 1024], BF16)

    eps_b = c_bias[:, BC_EPS:BC_EPS + 1]

    tk_c = []
    for dst, src in ((c_mask, maskT_in), (c_bdiag, bdiag_in), (c_idb, idb_in),
                     (c_idf, idf_in),
                     (c_onesT, onesT_in), (c_sel0, sel0_in), (c_sel, sel_in),
                     (c_bias, bias_in), (c_brb, brb_in), (c_oneb, oneb_in)):
        tk_c.append(p.dma(p.sy, p.xd,
                          lambda s, dst=dst, src=src: s.dma_start(dst[:], src[:])))
    (TK_MASK, TK_BDIAG, TK_IDB, TK_IDF, TK_ONES, TK_SEL0, TK_SEL, TK_BIAS,
     TK_BRB, TK_ONEB) = tk_c
    t_rows0 = p.op(p.gp, lambda g: g.memset(c_rows[:], 0.0))
    t_ms1 = p.op(p.gp, lambda g: g.memset(c_be2s[:], 0.0))
    t_be2 = p.dma(p.sy, p.xd, lambda s: s.dma_start(c_be2s[0:E, :], be2d[:, :]),
                  deps=[t_ms1])
    t_wrb = p.dma(p.gp, p.wdx, lambda g: g.dma_start(
        c_wrb.ap().rearrange("p (jc e) -> p jc e", jc=NF),
        wr.rearrange("(jc p) e -> p jc e", p=128)))

    NSLOT = 12
    slot_free = [None] * NSLOT
    ring_state = [0]

    def wload(dram_2d, c0, c1, njc, nslots, r0=0):
        pptr = ring_state[0]
        if pptr + nslots > NSLOT:
            pptr = 0
        ring_state[0] = (pptr + nslots) % NSLOT if pptr + nslots >= NSLOT else pptr + nslots
        deps = [slot_free[pptr + i] for i in range(nslots)]
        off = pptr * 1024
        ncols = c1 - c0
        dst = wring[:, off:off + njc * ncols].rearrange("p (jc c) -> p jc c", jc=njc)
        src = dram_2d[r0:r0 + njc * 128, c0:c1].rearrange("(jc p) c -> p jc c", p=128)
        tok = p.dma(p.gp, p.wds[pptr], lambda g, dst=dst, src=src: g.dma_start(dst, src),
                    deps=[d for d in deps if d is not None])
        return off, tok, pptr, nslots

    def wrelease(pptr, nslots, tok):
        for i in range(nslots):
            slot_free[pptr + i] = tok

    def emit_ln(src_fn, src_tok, ps4, sqpp, extra=None):
        rows = c_rows
        ps_sum, ps_sq, ps_sc, ps_b = ps4
        src_tok = src_tok if isinstance(src_tok, list) else [src_tok]
        sq_toks = [None] * NF
        mm_sq = [None] * NF
        last_sum = None
        for fc in range(NF):
            deps = src_tok + [TK_ONES]
            if fc == 0 and extra is not None:
                deps.append(extra)
            last_sum = p.op(p.pe, lambda t, fc=fc: t.matmul(
                ps_sum[:, :], c_onesT[:, :], src_fn(fc),
                start=(fc == 0), stop=(fc == NF - 1)), deps=deps)
        for fc in range(NF):
            dsq = list(src_tok)
            if fc < 2 and extra is not None:
                dsq.append(extra)
            if fc >= 2:
                dsq.append(mm_sq[fc - 2])
            sq_toks[fc] = p.op(p.ac, lambda a, fc=fc: a.activation(
                sqpp[:, (fc % 2) * NS:(fc % 2) * NS + NS], src_fn(fc), AF.Square),
                deps=dsq)
            dmm = [sq_toks[fc]]
            if fc == 0 and extra is not None:
                dmm.append(extra)
            mm_sq[fc] = p.op(p.pe, lambda t, fc=fc: t.matmul(
                ps_sq[:, :], c_onesT[:, :], sqpp[:, (fc % 2) * NS:(fc % 2) * NS + NS],
                start=(fc == 0), stop=(fc == NF - 1)), deps=dmm)
        t_sum, t_sq = last_sum, mm_sq[-1]
        ex = [extra] if extra is not None else []
        s1 = p.op(p.ve, lambda v: v.tensor_scalar(
            rows[0:1, 2 * NS:3 * NS], ps_sum[0:1, :], 1.0 / D, None,
            ALU.mult, ALU.bypass), deps=[t_sum] + ex)
        s2 = p.op(p.ve, lambda v: v.tensor_tensor(
            rows[0:1, 0:NS], rows[0:1, 2 * NS:3 * NS],
            rows[0:1, 2 * NS:3 * NS], ALU.mult), deps=[s1] + ex)
        s3 = p.op(p.ve, lambda v: v.tensor_scalar(
            ps_sc[0:1, :], ps_sq[0:1, :], 1.0 / D, None, ALU.mult, ALU.bypass),
            deps=[t_sq] + ex)
        s4 = p.op(p.ve, lambda v: v.tensor_tensor(
            ps_sq[0:1, :], ps_sc[0:1, :], rows[0:1, 0:NS], ALU.subtract),
            deps=[s2, s3])
        s5 = p.op(p.ac, lambda a: a.activation(
            ps_sc[0:1, :], ps_sq[0:1, :], AF.Sqrt, bias=eps_b[0:1, :]),
            deps=[s4, TK_BIAS])
        s6 = p.op(p.ve, lambda v: v.reciprocal(rows[0:1, 0:NS], ps_sc[0:1, :]),
                  deps=[s5, s4])
        s7 = p.op(p.ve, lambda v: v.tensor_tensor(
            rows[0:1, NS:2 * NS], rows[0:1, 2 * NS:3 * NS], rows[0:1, 0:NS],
            ALU.mult), deps=[s6] + ex)
        s8 = p.op(p.ve, lambda v: v.tensor_scalar(
            rows[0:1, NS:2 * NS], rows[0:1, NS:2 * NS], -1.0, None,
            ALU.mult, ALU.bypass), deps=[s7])
        t_rb = p.op(p.pe, lambda t: t.matmul(
            ps_sum[:, :], c_sel0[:, :], rows[:, 0:NS], start=True, stop=True),
            deps=[s6, TK_SEL0, t_rows0])
        t_bb = p.op(p.pe, lambda t: t.matmul(
            ps_b[:, :], c_sel0[:, :], rows[:, NS:2 * NS], start=True,
            stop=True), deps=[s8, s3])
        return t_rb, t_bb

    attn_scope = scope()
    attn_scope.__enter__()
    Sbf = sb("Sbf", [128, NF * 130], BF16)
    Sf32 = sb("Sf32", [128, NF * 128], F32)
    zf32 = sb("zf32", [128, NF], F32)
    x0bufs = [sb("x0a", [128, NF * QT], F32), sb("x0b", [128, NF * QT], F32)]
    t_szero = p.op(p.gp, lambda g: g.memset(Sbf[:], 0.0))
    t_s32z = p.op(p.gp, lambda g: g.memset(Sf32[:], 0.0))
    t_zz = p.op(p.gp, lambda g: g.memset(zf32[:], 0.0))
    t_x0_tok = {}
    t_x0_tok[0] = [
        p.dma(p.sy, p.xds[hh], lambda s, hh=hh: s.dma_start(
            x0bufs[0].ap().rearrange("p (fc t) -> p fc t", fc=NF)[:, hh * 4:hh * 4 + 4, :],
            xT.rearrange("(fc p) t -> p fc t", p=128)[:, hh * 4:hh * 4 + 4, 0:QT]))
        for hh in range(2)]
    if NQL > 1:
        t_x0_tok[1] = [p.dma(p.sy, p.xds[1], lambda s: s.dma_start(
            x0bufs[1].ap().rearrange("p (fc t) -> p fc t", fc=NF),
            xT.rearrange("(fc p) t -> p fc t", p=128)[:, :, QT:2 * QT]))]

    t_sbf_prev = {}
    t_zadd = {}
    t_c1prev = {ic: None for ic in range(NF)}
    st_x1 = []

    def _do_quarter(q):
        qsl = slice(q * QT, (q + 1) * QT)
        with scope():
            x0q = x0bufs[q % 2]
            sqpp = sb(f"sqpp{q}", [128, 2 * NS], F32)
            xnq = sb(f"xnq{q}", [128, NF * QT], BF16)
            qTq = sb(f"qTq{q}", [128, NF * QT], BF16)
            kTq = sb(f"kTq{q}", [128, NF * QT], BF16)
            vts = sb(f"vts{q}", [128, CPQ * D], BF16)
            kts = sb(f"kts{q}", [128, CPQ * D], BF16)
            eltmp = sb(f"eltmp{q}", [128, 2 * NS], F32)
            eltm2 = sb(f"eltm2{q}", [128, 2 * NS], F32)
            scs = sb(f"scs{q}", [128, 2 * CH], BF16)
            rcpq = sb(f"rcpq{q}", [128, 2], F32)
            ptok = sb(f"ptok{q}", [128, 2 * D], BF16)
            attnT = sb(f"attnT{q}", [128, NF * QT], BF16)
            yq = sb(f"yq{q}", [128, NF * QT], F32)
            psA, psB, psC, psD, psE, psF, psG = (
                g_psA, g_psB, g_psC, g_psD, g_psE, g_psF, g_psG)
            ptr = g_psT

            p.barrier([p.pe, p.ve, p.ac, p.sy])
            t_x0 = t_x0_tok[q]

            t_rb, t_bb = emit_ln(
                lambda fc: x0q[:, fc * QT:fc * QT + NS], t_x0,
                (psC, psD, psG, psD), sqpp)
            t_xn = []
            for fc in range(NF):
                sl = slice((fc % 2) * NS, (fc % 2) * NS + NS)
                ta = p.op(p.ve, lambda v, fc=fc, sl=sl: v.tensor_tensor(
                    eltmp[:, sl], x0q[:, fc * QT:fc * QT + NS], psC[:, :],
                    ALU.mult), deps=[t_rb])
                tb = p.op(p.ve, lambda v, fc=fc, sl=sl: v.tensor_tensor(
                    xnq[:, fc * QT:fc * QT + NS], eltmp[:, sl], psD[:, :],
                    ALU.add), deps=[ta, t_bb])
                t_xn.append(tb)
            t_xn_all = t_xn[-1]

            def featproj_elu(wT_dram, dstT, prev_bank):
                wins = [wload(wT_dram, w * 128, (w + 1) * 128, NF, 1)
                        for w in range(NF)]
                evac = []
                for ic in range(NF):
                    off, wtok, pp, nsl = wins[ic]
                    bank = psC if ic % 2 == 0 else psD
                    deps = [wtok, t_xn_all]
                    if ic >= 2:
                        deps.append(evac[ic - 2])
                    elif prev_bank[ic % 2] is not None:
                        deps.append(prev_bank[ic % 2])
                    mm = None
                    for jc in range(NF):
                        mm = p.op(p.pe, lambda t, off=off, jc=jc, bank=bank: t.matmul(
                            bank[:, :],
                            wring[:, off + jc * 128:off + (jc + 1) * 128],
                            xnq[:, jc * QT:jc * QT + NS],
                            start=(jc == 0), stop=(jc == NF - 1)),
                            deps=deps if jc == 0 else ())
                    wrelease(pp, nsl, mm)
                    sl = slice((ic % 2) * NS, (ic % 2) * NS + NS)
                    dte = [mm]
                    if ic >= 2:
                        dte.append(evac[ic - 2])
                    te = p.op(p.ac, lambda a, bank=bank, sl=sl: a.activation(
                        eltmp[:, sl], bank[:, :], AF.Exp), deps=dte)
                    tr2 = p.op(p.ac, lambda a, bank=bank, sl=sl: a.activation(
                        eltm2[:, sl], bank[:, :], AF.Relu), deps=[te])
                    tm = p.op(p.gp, lambda g, sl=sl: g.tensor_scalar(
                        eltmp[:, sl], eltmp[:, sl], 1.0, None, ALU.min, ALU.bypass),
                        deps=[te])
                    ta2 = p.op(p.gp, lambda g, ic=ic, sl=sl, dstT=dstT: g.tensor_tensor(
                        dstT[:, ic * QT:ic * QT + NS], eltmp[:, sl], eltm2[:, sl],
                        ALU.add), deps=[tm, tr2])
                    evac.append(ta2)
                return evac

            tq_toks = featproj_elu(wqT, qTq, (None, None))
            tk_toks = featproj_elu(wkT, kTq, (tq_toks[-2], tq_toks[-1]))

            vwins = [wload(wvT, w * 128, (w + 1) * 128, NF, 1) for w in range(NF)]
            v_toks = []
            for c8 in range(CPQ):
                for ni in range(2):
                    it = c8 * 2 + ni
                    bank = psC if it % 2 == 0 else psD
                    deps = [t_xn_all]
                    if it >= 2:
                        deps.append(v_toks[it - 2])
                    else:
                        deps.append(tk_toks[NF - 2 + it])
                    mm = None
                    nmm = 0
                    for wi in range(4 * ni, 4 * ni + 4):
                        off, wtok, pp, nsl = vwins[wi]
                        d2 = list(deps) + [wtok] if nmm == 0 else [wtok]
                        for jc in range(NF):
                            st_ = (nmm == 0)
                            sp_ = (nmm == 31)
                            mm = p.op(p.pe, lambda t, off=off, jc=jc, wi=wi, ni=ni,
                                      c8=c8, bank=bank, st_=st_, sp_=sp_: t.matmul(
                                bank[:, (wi - 4 * ni) * 128:(wi - 4 * ni + 1) * 128],
                                xnq[:, jc * QT + c8 * 128:jc * QT + (c8 + 1) * 128],
                                wring[:, off + jc * 128:off + (jc + 1) * 128],
                                start=st_, stop=sp_),
                                deps=d2 if jc == 0 else ())
                            nmm += 1
                    tv = p.op(p.ac, lambda a, c8=c8, ni=ni, bank=bank: a.activation(
                        vts[:, c8 * D + ni * NS:c8 * D + ni * NS + NS], bank[:, :],
                        AF.Copy), deps=[mm])
                    v_toks.append(tv)
                    if c8 == CPQ - 1:
                        v_last_mm = mm
            for wi in range(NF):
                off, wtok, pp, nsl = vwins[wi]
                wrelease(pp, nsl, v_last_mm)
            vall = v_toks[-1]

            kt_toks = []
            kt_prev_ev = None
            for c8 in range(CPQ):
                for icp in range(0, NF, 2):
                    deps = [tk_toks[icp], tk_toks[icp + 1], TK_IDB]
                    if kt_prev_ev is not None:
                        deps.append(kt_prev_ev)
                    p.op(p.pe, lambda t, c8=c8, ic=icp: t.transpose(
                        ptr[:, 0:128],
                        kTq[:, ic * QT + c8 * 128:ic * QT + (c8 + 1) * 128],
                        c_idb[:, :]), deps=deps)
                    tt = p.op(p.pe, lambda t, c8=c8, ic=icp + 1: t.transpose(
                        ptr[:, 128:256],
                        kTq[:, ic * QT + c8 * 128:ic * QT + (c8 + 1) * 128],
                        c_idb[:, :]))
                    tc = p.op(p.ve, lambda v, c8=c8, ic=icp: v.tensor_copy(
                        kts[:, c8 * D + ic * 128:c8 * D + (ic + 2) * 128],
                        ptr[:, 0:256]), deps=[tt])
                    kt_prev_ev = tc
                    kt_toks.extend([tc, tc])

            t_sc_ev = {}
            t_nd_pe = {}
            t_div = {}
            t_at_ev = {}
            t_tok_tr = {}
            t_div_last = None
            for c8 in range(CPQ):
                cglob = q * CPQ + c8
                for ic in range(NF):
                    key = (c8, ic)
                    ndb = psC if ic % 2 == 0 else psD
                    qc = slice(ic * QT + c8 * 128, ic * QT + (c8 + 1) * 128)
                    s0 = ic * 130
                    pk = (c8, ic - 2) if ic >= 2 else \
                        ((c8 - 1, NF - 2 + ic) if c8 > 0 else None)
                    pk1 = (c8, ic - 1) if ic >= 1 else \
                        ((c8 - 1, NF - 1) if c8 > 0 else None)

                    deps = [tq_toks[ic], tk_toks[ic]]
                    if pk1 is not None:
                        deps.append(t_sc_ev[pk1])
                    t_s0 = p.op(p.pe, lambda t, qc=qc: t.matmul(
                        psE[:, 0:128], kTq[0:64, qc], qTq[0:64, qc],
                        start=True, stop=True), deps=deps)
                    t_s1 = p.op(p.pe, lambda t, qc=qc: t.matmul(
                        psF[:, 0:128], kTq[64:128, qc], qTq[64:128, qc],
                        start=True, stop=True), deps=[t_s0])

                    ev0 = p.op(p.ve, lambda v: v.tensor_tensor(
                        scs[:, 0:CH], psE[:, 0:128], c_mask[:, :], ALU.mult),
                        deps=[t_s1, TK_MASK])
                    ev1 = p.op(p.ve, lambda v: v.tensor_tensor(
                        scs[:, CH:2 * CH], psF[:, 0:128], c_mask[:, :], ALU.mult),
                        deps=[ev0])
                    t_sc_ev[key] = ev1

                    use_sbf = cglob > 0
                    deps_nd = [ev1, vall, TK_ONEB]
                    if use_sbf:
                        deps_nd += t_sbf_prev[ic]
                    if pk is not None:
                        deps_nd.append(t_div[pk])
                    else:
                        deps_nd.append(v_toks[6 + (ic % 2)])
                    mm = None
                    imm = 0
                    nmm_tot = 8 if use_sbf else 4
                    for hh in range(2):
                        hsl = slice(hh * 64, (hh + 1) * 64)
                        if use_sbf:
                            st_ = (imm == 0)
                            mm = p.op(p.pe, lambda t, qc=qc, hsl=hsl, ndb=ndb,
                                      s0=s0, hh=hh, st_=st_: t.matmul(
                                ndb[:, hsl], qTq[:, qc],
                                Sbf[:, s0 + hh * 64:s0 + (hh + 1) * 64],
                                start=st_, stop=False),
                                deps=deps_nd if imm == 0 else ())
                            imm += 1
                            mm = p.op(p.pe, lambda t, qc=qc, ndb=ndb, s0=s0,
                                      hh=hh: t.matmul(
                                ndb[:, 128 + hh:129 + hh], qTq[:, qc],
                                Sbf[:, s0 + 128 + hh:s0 + 129 + hh],
                                start=False, stop=False))
                            imm += 1
                        st_ = (imm == 0)
                        mm = p.op(p.pe, lambda t, hh=hh, hsl=hsl, ndb=ndb,
                                  c8=c8, ic=ic, st_=st_: t.matmul(
                            ndb[:, hsl], scs[:, hh * CH:(hh + 1) * CH],
                            vts[:, c8 * D + ic * 128 + hh * 64:
                                 c8 * D + ic * 128 + hh * 64 + 64],
                            start=st_, stop=False),
                            deps=deps_nd if imm == 0 else ())
                        imm += 1
                        sp_ = (hh == 1)
                        mm = p.op(p.pe, lambda t, hh=hh, ndb=ndb, sp_=sp_: t.matmul(
                            ndb[:, 128 + hh:129 + hh], scs[:, hh * CH:(hh + 1) * CH],
                            c_oneb[:, :], start=False, stop=sp_))
                        imm += 1
                    t_nd = mm
                    t_nd_pe[key] = t_nd

                    r1 = p.op(p.ve, lambda v, ndb=ndb: v.reciprocal(
                        rcpq[:, 0:2], ndb[:, 128:130]),
                        deps=[t_nd, t_div_last])
                    dv0 = p.op(p.ac, lambda a, ndb=ndb, c8=c8, ic=ic: a.activation(
                        ptok[:, (c8 % 2) * D + ic * 128:(c8 % 2) * D + ic * 128 + 64],
                        ndb[:, 0:64], AF.Copy, bias=EPS_AT, scale=rcpq[:, 0:1]),
                        deps=[r1, t_tok_tr.get((c8 - 2, ic))])
                    dv1 = p.op(p.ac, lambda a, ndb=ndb, c8=c8, ic=ic: a.activation(
                        ptok[:, (c8 % 2) * D + ic * 128 + 64:
                             (c8 % 2) * D + (ic + 1) * 128],
                        ndb[:, 64:128], AF.Copy, bias=EPS_AT, scale=rcpq[:, 1:2]),
                        deps=[dv0])
                    t_div[key] = dv1
                    t_div_last = dv1

                for ic in range(NF):
                    dsb = psA if ic % 2 == 0 else psB
                    vc = slice(c8 * D + ic * 128, c8 * D + (ic + 1) * 128)
                    s0 = ic * 130
                    f0 = ic * 128
                    key = (c8, ic)
                    pk = (c8, ic - 2) if ic >= 2 else \
                        ((c8 - 1, NF - 2 + ic) if c8 > 0 else None)
                    deps_ds = [kt_toks[c8 * NF + ic], vall]
                    if pk is not None:
                        deps_ds.append(t_zadd.get(pk))
                    m1_ = p.op(p.pe, lambda t, dsb=dsb, vc=vc: t.matmul(
                        dsb[:, 0:128], kts[:, vc], vts[:, vc],
                        start=True, stop=False), deps=deps_ds)
                    t_ds = p.op(p.pe, lambda t, dsb=dsb, vc=vc: t.matmul(
                        dsb[:, 128:129], kts[:, vc], c_oneb[:, :],
                        start=False, stop=True))
                    da = [t_ds]
                    if t_c1prev[ic] is not None:
                        da.append(t_c1prev[ic])
                    else:
                        da += [t_s32z, t_zz]
                    a1 = p.op(p.ve, lambda v, dsb=dsb, f0=f0: v.tensor_tensor(
                        Sf32[0:64, f0:f0 + 64], Sf32[0:64, f0:f0 + 64],
                        dsb[0:64, 0:64], ALU.add), deps=da)
                    a2 = p.op(p.ve, lambda v, dsb=dsb, f0=f0: v.tensor_tensor(
                        Sf32[64:128, f0 + 64:f0 + 128],
                        Sf32[64:128, f0 + 64:f0 + 128],
                        dsb[64:128, 64:128], ALU.add), deps=[a1])
                    za = p.op(p.ve, lambda v, dsb=dsb, ic=ic: v.tensor_tensor(
                        zf32[:, ic:ic + 1], zf32[:, ic:ic + 1],
                        dsb[:, 128:129], ALU.add), deps=[a2])
                    dc1 = [a2, t_nd_pe[key]]
                    if cglob == 0:
                        dc1.append(t_szero)
                    c1 = p.op(p.ac, lambda a, f0=f0, s0=s0: a.activation(
                        Sbf[:, s0:s0 + 128], Sf32[:, f0:f0 + 128], AF.Copy),
                        deps=dc1)
                    z0 = p.op(p.ac, lambda a, ic=ic, s0=s0: a.activation(
                        Sbf[0:64, s0 + 128:s0 + 129], zf32[0:64, ic:ic + 1],
                        AF.Copy), deps=[za, c1])
                    z1 = p.op(p.ac, lambda a, ic=ic, s0=s0: a.activation(
                        Sbf[64:128, s0 + 129:s0 + 130], zf32[64:128, ic:ic + 1],
                        AF.Copy), deps=[z0])
                    t_zadd[key] = za
                    t_c1prev[ic] = z1
                    t_sbf_prev[ic] = [z1]

                for fcp in range(0, NF, 2):
                    j = c8 * NF + fcp
                    deps = [t_div[(c8, fcp)], t_div[(c8, fcp + 1)], TK_IDB,
                            kt_prev_ev]
                    if (c8, fcp) != (0, 0):
                        deps.append(at_prev_ev)
                    tt0 = p.op(p.pe, lambda t, c8=c8, fc=fcp: t.transpose(
                        ptr[:, 0:128],
                        ptok[:, (c8 % 2) * D + fc * 128:(c8 % 2) * D + (fc + 1) * 128],
                        c_idb[:, :]), deps=deps)
                    tt = p.op(p.pe, lambda t, c8=c8, fc=fcp + 1: t.transpose(
                        ptr[:, 128:256],
                        ptok[:, (c8 % 2) * D + fc * 128:(c8 % 2) * D + (fc + 1) * 128],
                        c_idb[:, :]))
                    tc0 = p.op(p.ve, lambda v, c8=c8, fc=fcp: v.tensor_copy(
                        attnT[:, fc * QT + c8 * 128:fc * QT + (c8 + 1) * 128],
                        ptr[:, 0:128]), deps=[tt])
                    tc = p.op(p.ve, lambda v, c8=c8, fc=fcp + 1: v.tensor_copy(
                        attnT[:, fc * QT + c8 * 128:fc * QT + (c8 + 1) * 128],
                        ptr[:, 128:256]), deps=[tc0])
                    at_prev_ev = tc
                    t_at_ev[j] = tc
                    t_at_ev[j + 1] = tc
                    t_tok_tr[(c8, fcp)] = tt0
                    t_tok_tr[(c8, fcp + 1)] = tt
            t_attnT_all = t_at_ev[CPQ * NF - 1]

            owins = [wload(woT, w * 128, (w + 1) * 128, NF, 1) for w in range(NF)]
            y_toks = []
            for ic in range(NF):
                off, wtok, pp, nsl = owins[ic]
                bank = psC if ic % 2 == 0 else psD
                deps = [wtok, t_attnT_all]
                if ic >= 2:
                    deps.append(y_toks[ic - 2])
                mm = None
                for jc in range(NF):
                    mm = p.op(p.pe, lambda t, off=off, jc=jc, bank=bank: t.matmul(
                        bank[:, :], wring[:, off + jc * 128:off + (jc + 1) * 128],
                        attnT[:, jc * QT:jc * QT + NS],
                        start=(jc == 0), stop=(jc == NF - 1)),
                        deps=deps if jc == 0 else ())
                wrelease(pp, nsl, mm)
                ty = p.op(p.ac, lambda a, ic=ic, bank=bank: a.activation(
                    yq[:, ic * QT:ic * QT + NS], bank[:, :], AF.Copy), deps=[mm])
                y_toks.append(ty)
            t_y_all = y_toks[-1]

            t_rb2, t_bb2 = emit_ln(
                lambda fc: yq[:, fc * QT:fc * QT + NS], t_y_all,
                (psC, psD, psG, psD), sqpp, extra=t_attnT_all)
            x1_toks = []
            for fc in range(NF):
                sl = slice((fc % 2) * NS, (fc % 2) * NS + NS)
                tm = p.op(p.ve, lambda v, fc=fc, sl=sl: v.tensor_tensor(
                    eltmp[:, sl], yq[:, fc * QT:fc * QT + NS], psC[:, :],
                    ALU.mult), deps=[t_rb2])
                tn = p.op(p.ve, lambda v, fc=fc, sl=sl: v.tensor_tensor(
                    eltmp[:, sl], eltmp[:, sl], psD[:, :], ALU.add),
                    deps=[tm, t_bb2])
                to = p.op(p.ve, lambda v, fc=fc, sl=sl: v.tensor_tensor(
                    x0q[:, fc * QT:fc * QT + NS], x0q[:, fc * QT:fc * QT + NS],
                    eltmp[:, sl], ALU.add), deps=[tn])
                x1_toks.append(to)
            dst = out if PHASE_LIMIT == 1 else x1s
            t_st = p.dma(p.sy, p.st, lambda s, dst=dst, qsl=qsl: s.dma_start(
                dst.rearrange("(fc p) t -> p fc t", p=128)[:, :, qsl],
                x0q.ap().rearrange("p (fc t) -> p fc t", fc=NF)),
                deps=x1_toks)
            st_x1.append(t_st)
            if q + 2 < NQL:
                t_x0_tok[q + 2] = [p.dma(p.sy, p.xds[q % 2], lambda s, q=q: s.dma_start(
                    x0bufs[q % 2].ap().rearrange("p (fc t) -> p fc t", fc=NF),
                    xT.rearrange("(fc p) t -> p fc t", p=128)[:, :, (q + 2) * QT:(q + 3) * QT]),
                    deps=[t_st])]

    for q in range(NQL):
        _do_quarter(q)
    attn_scope.__exit__(None, None, None)

    def _do_half(hf):
        hsl = slice(hf * HT, (hf + 1) * HT)
        with scope():
            x1h = sb(f"x1h{hf}", [128, NF * HT], F32)
            sqpp = sb(f"fsq{hf}", [128, 2 * NS], F32)
            xn2 = sb(f"xn2{hf}", [128, NF * HT], BF16)
            yr = sb(f"yr{hf}", [128, NF * HT], F32)
            act = sb(f"act{hf}", [128, 24 * HT], BF16)
            gact = sb(f"gact{hf}", [128, 2 * HT], BF16)
            aval = sb(f"aval{hf}", [128, 2 * HT], BF16)
            wTt = sb(f"wT{hf}", [128, HT], F32)
            wtok = sb(f"wtok{hf}", [128, 128], F32)
            wb = sb(f"wb{hf}", [128, HT], F32)
            rtr = sb(f"rtr{hf}", [128, 160], F32)
            tsc = sb(f"tsc{hf}", [128, 2 * NS], F32)
            psA, psB, psC, psD, psG = g_psA, g_psB, g_psC, g_psD, g_psG
            ptrf = g_psT.ap().bitcast(F32)

            p.barrier([p.pe, p.ve, p.ac, p.sy])
            t_wz = p.op(p.gp, lambda g: g.memset(wtok[:], 0.0))

            t_x1 = p.dma(p.sy, p.xds[hf % 2], lambda s, hsl=hsl: s.dma_start(
                x1h.ap().rearrange("p (fc t) -> p fc t", fc=NF),
                x1s.rearrange("(fc p) t -> p fc t", p=128)[:, :, hsl]),
                deps=[st_x1[2 * hf], st_x1[2 * hf + 1]])

            t_xn2 = []
            prev_norm = None
            for ns_i in range(2):
                t_rb3, t_bb3 = emit_ln(
                    lambda fc, ns_i=ns_i: x1h[:, fc * HT + ns_i * NS:
                                              fc * HT + ns_i * NS + NS],
                    t_x1, (psA, psB, psC, psD), sqpp, extra=prev_norm)
                for fc in range(NF):
                    sl = slice((fc % 2) * NS, (fc % 2) * NS + NS)
                    ta = p.op(p.ve, lambda v, fc=fc, ns_i=ns_i, sl=sl: v.tensor_tensor(
                        tsc[:, sl],
                        x1h[:, fc * HT + ns_i * NS:fc * HT + ns_i * NS + NS],
                        psA[:, :], ALU.mult), deps=[t_rb3])
                    tb = p.op(p.ve, lambda v, fc=fc, ns_i=ns_i, sl=sl: v.tensor_tensor(
                        xn2[:, fc * HT + ns_i * NS:fc * HT + ns_i * NS + NS],
                        tsc[:, sl], psD[:, :], ALU.add), deps=[ta, t_bb3])
                    t_xn2.append(tb)
                prev_norm = t_xn2[-1]
            t_xn2_all = t_xn2[-1]

            t_wT = []
            t_wtr = None
            t_rtr_prev = None
            for c8 in range(NF):
                mm = None
                for jc in range(NF):
                    d0 = [t_xn2_all, t_wrb]
                    if t_rtr_prev is not None:
                        d0.append(t_rtr_prev)
                    mm = p.op(p.pe, lambda t, c8=c8, jc=jc: t.matmul(
                        psG[:, 0:E],
                        xn2[:, jc * HT + c8 * 128:jc * HT + (c8 + 1) * 128],
                        c_wrb[:, jc * E:(jc + 1) * E],
                        start=(jc == 0), stop=(jc == NF - 1)),
                        deps=d0 if jc == 0 else ())
                lg = p.op(p.ve, lambda v: v.tensor_tensor(
                    rtr[:, 0:E], psG[:, 0:E], c_brb[:, :], ALU.add),
                    deps=[mm, TK_BRB, t_rtr_prev])
                m1 = p.op(p.ve, lambda v: v.tensor_reduce(
                    rtr[:, 32:33], rtr[:, 0:E], AX.X, ALU.max), deps=[lg])
                k1 = p.op(p.ve, lambda v: v.tensor_scalar(
                    rtr[:, 48:48 + E], rtr[:, 0:E], rtr[:, 32:33], None,
                    ALU.is_equal, ALU.bypass), deps=[m1])
                k2 = p.op(p.ve, lambda v: v.tensor_scalar(
                    rtr[:, 64:64 + E], rtr[:, 48:48 + E], -1e30, None,
                    ALU.mult, ALU.bypass), deps=[k1])
                l2 = p.op(p.ve, lambda v: v.tensor_tensor(
                    rtr[:, 64:64 + E], rtr[:, 0:E], rtr[:, 64:64 + E],
                    ALU.add), deps=[k2])
                m2 = p.op(p.ve, lambda v: v.tensor_reduce(
                    rtr[:, 33:34], rtr[:, 64:64 + E], AX.X, ALU.max),
                    deps=[l2])
                k3 = p.op(p.ve, lambda v: v.tensor_scalar(
                    rtr[:, 80:80 + E], rtr[:, 64:64 + E], rtr[:, 33:34], None,
                    ALU.is_equal, ALU.bypass), deps=[m2])
                dd = p.op(p.ve, lambda v: v.tensor_tensor(
                    rtr[:, 34:35], rtr[:, 32:33], rtr[:, 33:34],
                    ALU.subtract), deps=[m2])
                w1 = p.op(p.ac, lambda a: a.activation(
                    rtr[:, 35:36], rtr[:, 34:35], AF.Sigmoid, scale=1.0 / TAU),
                    deps=[dd])
                w2 = p.op(p.ve, lambda v: v.tensor_scalar(
                    rtr[:, 36:37], rtr[:, 35:36], -1.0, 1.0, ALU.mult,
                    ALU.add), deps=[w1])
                q1 = p.op(p.ve, lambda v: v.tensor_scalar(
                    rtr[:, 96:96 + E], rtr[:, 48:48 + E], rtr[:, 35:36], None,
                    ALU.mult, ALU.bypass), deps=[w2])
                q2 = p.op(p.ve, lambda v: v.tensor_scalar(
                    rtr[:, 112:112 + E], rtr[:, 80:80 + E], rtr[:, 36:37],
                    None, ALU.mult, ALU.bypass), deps=[q1])
                wt = p.op(p.ve, lambda v: v.tensor_tensor(
                    wtok[:, 0:E], rtr[:, 96:96 + E], rtr[:, 112:112 + E],
                    ALU.add), deps=[q2, t_wz, t_wtr])
                ttr = p.op(p.pe, lambda t: t.transpose(
                    ptrf[:, 0:128], wtok[:, :], c_idf[:, :]),
                    deps=[wt, TK_IDF] + ([t_wT[-1]] if t_wT else []))
                tev = p.op(p.ve, lambda v, c8=c8: v.tensor_copy(
                    wTt[:, c8 * 128:(c8 + 1) * 128], ptrf[:, 0:128]), deps=[ttr])
                t_wtr = ttr
                t_rtr_prev = wt
                t_wT.append(tev)
            t_wT_all = t_wT[-1]

            t_acts = {}
            for sc in range(16):
                wa = wload(ws1, sc * 128, (sc + 1) * 128, NF, 1)
                wg = wload(ws1, HS + sc * 128, HS + (sc + 1) * 128, NF, 1)
                offa, toka, ppa, nsa = wa
                offg, tokg, ppg, nsg = wg
                for ns_i in range(2):
                    it = sc * 2 + ns_i
                    ba = psA if it % 2 == 0 else psC
                    bg = psB if it % 2 == 0 else psD
                    deps = [toka, t_xn2_all]
                    if it >= 2:
                        deps.append(t_acts[it - 2])
                    else:
                        deps.append(t_wT_all)
                    mma = None
                    for jc in range(NF):
                        mma = p.op(p.pe, lambda t, offa=offa, jc=jc, ba=ba,
                                   ns_i=ns_i: t.matmul(
                            ba[:, :],
                            wring[:, offa + jc * 128:offa + (jc + 1) * 128],
                            xn2[:, jc * HT + ns_i * NS:jc * HT + ns_i * NS + NS],
                            start=(jc == 0), stop=(jc == NF - 1)),
                            deps=deps if jc == 0 else ())
                    mmg = None
                    for jc in range(NF):
                        mmg = p.op(p.pe, lambda t, offg=offg, jc=jc, bg=bg,
                                   ns_i=ns_i: t.matmul(
                            bg[:, :],
                            wring[:, offg + jc * 128:offg + (jc + 1) * 128],
                            xn2[:, jc * HT + ns_i * NS:jc * HT + ns_i * NS + NS],
                            start=(jc == 0), stop=(jc == NF - 1)),
                            deps=[tokg] if jc == 0 else ())
                    ta_ = p.op(p.ac, lambda a, ba=ba, sc=sc, it=it: a.activation(
                        aval[:, (it % 2) * HT:(it % 2) * HT + NS], ba[:, :],
                        AF.Identity, bias=c_bias[:, BC_BS1 + sc:BC_BS1 + sc + 1]),
                        deps=[mma, TK_BIAS])
                    tg_ = p.op(p.ac, lambda a, bg=bg, sc=sc, it=it: a.activation(
                        gact[:, (it % 2) * HT:(it % 2) * HT + NS], bg[:, :],
                        AF.Gelu,
                        bias=c_bias[:, BC_BS1 + 16 + sc:BC_BS1 + 16 + sc + 1]),
                        deps=[mmg])
                    tm_ = p.op(p.ve, lambda v, sc=sc, ns_i=ns_i, it=it: v.tensor_tensor(
                        act[:, sc * HT + ns_i * NS:sc * HT + ns_i * NS + NS],
                        aval[:, (it % 2) * HT:(it % 2) * HT + NS],
                        gact[:, (it % 2) * HT:(it % 2) * HT + NS], ALU.mult),
                        deps=[ta_, tg_])
                    t_acts[it] = tm_
                    if ns_i == 1:
                        wrelease(ppa, nsa, mma)
                        wrelease(ppg, nsg, mmg)
            t_acts_all = t_acts[31]

            t_yr = {}
            t_bw_prev = None
            for dc in range(NF):
                w2w = wload(ws2, dc * 128, (dc + 1) * 128, 16, 2)
                off2, tok2, pp2, ns2 = w2w
                for ns_i in range(2):
                    it = dc * 2 + ns_i
                    bank = psA if it % 2 == 0 else psB
                    deps = [tok2, t_acts_all]
                    if it >= 2:
                        deps.append(t_yr[it - 2])
                    mm = None
                    for sc in range(16):
                        mm = p.op(p.pe, lambda t, off2=off2, sc=sc, bank=bank,
                                  ns_i=ns_i: t.matmul(
                            bank[:, :],
                            wring[:, off2 + sc * 128:off2 + (sc + 1) * 128],
                            act[:, sc * HT + ns_i * NS:sc * HT + ns_i * NS + NS],
                            start=(sc == 0), stop=(sc == 15)),
                            deps=deps if sc == 0 else ())
                    mw = p.op(p.pe, lambda t, dc=dc, ns_i=ns_i: t.matmul(
                        psG[:, :], c_be2s[:, dc * 128:(dc + 1) * 128],
                        wTt[:, ns_i * NS:ns_i * NS + NS], start=True,
                        stop=True),
                        deps=[t_wT_all, t_be2,
                              (t_bw_prev if t_bw_prev is not None else t_acts_all)])
                    ty1 = p.op(p.ac, lambda a, dc=dc, ns_i=ns_i, bank=bank: a.activation(
                        yr[:, dc * HT + ns_i * NS:dc * HT + ns_i * NS + NS],
                        bank[:, :], AF.Identity,
                        bias=c_bias[:, BC_BS2 + dc:BC_BS2 + dc + 1]),
                        deps=[mm])
                    ty2 = p.op(p.ve, lambda v, dc=dc, ns_i=ns_i: v.tensor_tensor(
                        yr[:, dc * HT + ns_i * NS:dc * HT + ns_i * NS + NS],
                        yr[:, dc * HT + ns_i * NS:dc * HT + ns_i * NS + NS],
                        psG[:, :], ALU.add), deps=[ty1, mw])
                    t_yr[it] = ty2
                    t_bw_prev = ty2
                wrelease(pp2, ns2, t_yr[dc * 2 + 1])
            t_yr_all = t_yr[15]

            t_prev_exp_yr = None
            t_wb_war = None
            for e in range(E):
                t_wb = []
                for ns_i in range(2):
                    mb = p.op(p.pe, lambda t, e=e, ns_i=ns_i: t.matmul(
                        psG[:, :], c_sel[:, e * 128:(e + 1) * 128],
                        wTt[:, ns_i * NS:ns_i * NS + NS],
                        start=True, stop=True),
                        deps=[t_wT_all, TK_SEL,
                              (t_wb_war if t_wb_war is not None else t_yr_all)])
                    ev = p.op(p.ve, lambda v, ns_i=ns_i: v.tensor_copy(
                        wb[:, ns_i * NS:ns_i * NS + NS], psG[:, :]),
                        deps=[mb, (t_prev_exp_yr if t_prev_exp_yr is not None
                                   else t_yr_all)])
                    t_wb.append(ev)
                    t_wb_war = ev
                t_wb_all = t_wb[-1]

                t_eacts = {}
                wa = wg = None
                for sc in range(24):
                    if sc % 2 == 0:
                        wa = wload(we1f, sc * 128, (sc + 2) * 128, NF, 2, r0=e * D)
                        wg = wload(we1f, HR + sc * 128, HR + (sc + 2) * 128, NF, 2, r0=e * D)
                    offa, toka, ppa, nsa = wa
                    offg, tokg, ppg, nsg = wg
                    sub = sc % 2
                    for ns_i in range(2):
                        it = sc * 2 + ns_i
                        ba = psA if it % 2 == 0 else psC
                        bg = psB if it % 2 == 0 else psD
                        deps = [toka, t_xn2_all]
                        if it >= 2:
                            deps.append(t_eacts[it - 2])
                        elif e > 0:
                            deps.append(t_prev_exp_Bmm)
                        else:
                            deps.append(t_yr_all)
                        mma = None
                        for jc in range(NF):
                            mma = p.op(p.pe, lambda t, offa=offa, jc=jc, ba=ba,
                                       ns_i=ns_i, sub=sub: t.matmul(
                                ba[:, :],
                                wring[:, offa + jc * 256 + sub * 128:
                                       offa + jc * 256 + sub * 128 + 128],
                                xn2[:, jc * HT + ns_i * NS:
                                    jc * HT + ns_i * NS + NS],
                                start=(jc == 0), stop=(jc == NF - 1)),
                                deps=deps if jc == 0 else ())
                        mmg = None
                        for jc in range(NF):
                            mmg = p.op(p.pe, lambda t, offg=offg, jc=jc, bg=bg,
                                       ns_i=ns_i, sub=sub: t.matmul(
                                bg[:, :],
                                wring[:, offg + jc * 256 + sub * 128:
                                       offg + jc * 256 + sub * 128 + 128],
                                xn2[:, jc * HT + ns_i * NS:
                                    jc * HT + ns_i * NS + NS],
                                start=(jc == 0), stop=(jc == NF - 1)),
                                deps=[tokg] if jc == 0 else ())
                        bca = BC_BE1 + e * 48 + sc
                        bcg = BC_BE1 + e * 48 + 24 + sc
                        ta_ = p.op(p.ac, lambda a, ba=ba, it=it, bca=bca: a.activation(
                            aval[:, (it % 2) * HT:(it % 2) * HT + NS], ba[:, :],
                            AF.Identity, bias=c_bias[:, bca:bca + 1]), deps=[mma])
                        tg_ = p.op(p.ac, lambda a, bg=bg, it=it, bcg=bcg: a.activation(
                            gact[:, (it % 2) * HT:(it % 2) * HT + NS], bg[:, :],
                            AF.Gelu, bias=c_bias[:, bcg:bcg + 1]), deps=[mmg])
                        tm_ = p.op(p.ve, lambda v, sc=sc, ns_i=ns_i, it=it: v.tensor_tensor(
                            act[:, sc * HT + ns_i * NS:sc * HT + ns_i * NS + NS],
                            aval[:, (it % 2) * HT:(it % 2) * HT + NS],
                            gact[:, (it % 2) * HT:(it % 2) * HT + NS],
                            ALU.mult), deps=[ta_, tg_])
                        t_eacts[it] = tm_
                        if sub == 1 and ns_i == 1:
                            wrelease(ppa, nsa, mma)
                            wrelease(ppg, nsg, mmg)
                t_pa = t_eacts[47]

                t_yra = {}
                for dc in range(NF):
                    w2e = wload(we2f, dc * 128, (dc + 1) * 128, 24, 3, r0=e * HR)
                    off2, tok2, pp2, ns2 = w2e
                    for ns_i in range(2):
                        it = dc * 2 + ns_i
                        bank = psC if it % 2 == 0 else psD
                        deps = [tok2, t_pa, t_wb_all]
                        if it >= 2:
                            deps.append(t_yra[it - 2])
                        elif e > 0:
                            deps.append(t_prev_exp_B.get(it))
                        else:
                            deps.append(t_xn2_all)
                        mm = None
                        for sc in range(24):
                            mm = p.op(p.pe, lambda t, off2=off2, sc=sc,
                                      bank=bank, ns_i=ns_i: t.matmul(
                                bank[:, :],
                                wring[:, off2 + sc * 128:off2 + (sc + 1) * 128],
                                act[:, sc * HT + ns_i * NS:
                                    sc * HT + ns_i * NS + NS],
                                start=(sc == 0), stop=(sc == 23)),
                                deps=deps if sc == 0 else ())
                        tw = p.op(p.ve, lambda v, it=it, bank=bank, ns_i=ns_i: v.tensor_tensor(
                            tsc[:, (it % 2) * NS:(it % 2) * NS + NS],
                            bank[:, :], wb[:, ns_i * NS:ns_i * NS + NS],
                            ALU.mult), deps=[mm])
                        ta2 = p.op(p.ve, lambda v, dc=dc, ns_i=ns_i, it=it: v.tensor_tensor(
                            yr[:, dc * HT + ns_i * NS:dc * HT + ns_i * NS + NS],
                            yr[:, dc * HT + ns_i * NS:dc * HT + ns_i * NS + NS],
                            tsc[:, (it % 2) * NS:(it % 2) * NS + NS],
                            ALU.add),
                            deps=[tw, (t_yr[it] if e == 0 else None)])
                        t_yra[it] = ta2
                    wrelease(pp2, ns2, t_yra[dc * 2 + 1])
                t_prev_exp_B = t_yra
                t_prev_exp_Bmm = mm
                t_prev_exp_yr = t_yra[15]

            fin_toks = []
            prev_norm = t_prev_exp_yr
            for ns_i in range(2):
                t_rb4, t_bb4 = emit_ln(
                    lambda fc, ns_i=ns_i: yr[:, fc * HT + ns_i * NS:
                                             fc * HT + ns_i * NS + NS],
                    t_prev_exp_yr, (psA, psB, psC, psD), sqpp,
                    extra=prev_norm)
                for fc in range(NF):
                    sl = slice((fc % 2) * NS, (fc % 2) * NS + NS)
                    tm = p.op(p.ve, lambda v, fc=fc, ns_i=ns_i, sl=sl: v.tensor_tensor(
                        tsc[:, sl],
                        yr[:, fc * HT + ns_i * NS:fc * HT + ns_i * NS + NS],
                        psA[:, :], ALU.mult), deps=[t_rb4, t_prev_exp_yr])
                    tn = p.op(p.ve, lambda v, fc=fc, ns_i=ns_i, sl=sl: v.tensor_tensor(
                        tsc[:, sl], tsc[:, sl], psD[:, :], ALU.add),
                        deps=[tm, t_bb4])
                    to = p.op(p.ve, lambda v, fc=fc, ns_i=ns_i, sl=sl: v.tensor_tensor(
                        x1h[:, fc * HT + ns_i * NS:fc * HT + ns_i * NS + NS],
                        x1h[:, fc * HT + ns_i * NS:fc * HT + ns_i * NS + NS],
                        tsc[:, sl], ALU.add), deps=[tn])
                    fin_toks.append(to)
                prev_norm = fin_toks[-1]
            t_sto = p.dma(p.sy, p.st, lambda s, hsl=hsl: s.dma_start(
                out.rearrange("(fc p) t -> p fc t", p=128)[:, :, hsl],
                x1h.ap().rearrange("p (fc t) -> p fc t", fc=NF)),
                deps=fin_toks)
            st_x1.append(t_sto)

    if PHASE_LIMIT >= 2:
        for hf in range(2):
            _do_half(hf)

    for tkn in st_x1:
        p.wait(p.sy, tkn)

    p.finalize()
    sems = []
    for e in (p.pe, p.ve, p.ac, p.gp, p.sy, p.wdx, p.xd, p.st, *p.wds, *p.xds):
        cm = nc.semaphore(f"sem_{e.name}")
        e.sem = cm.__enter__()
        sems.append(cm)
    blk = nc.Block()
    block = blk.__enter__()

    @block.tensor
    def _(t):
        p.replay(p.pe, t)

    @block.vector
    def _(v):
        p.replay(p.ve, v)

    @block.scalar
    def _(a):
        p.replay(p.ac, a)

    @block.gpsimd
    def _(g):
        p.replay(p.gp, g)

    @block.sync
    def _(s):
        p.replay(p.sy, s)

    blk.__exit__(None, None, None)
    for cm in reversed(sems):
        cm.__exit__(None, None, None)
    while ctx:
        ctx.pop().__exit__(None, None, None)
    return nc


_CACHED = {}

_SEL0 = np.zeros((128, 128), dtype=np.float32)
_SEL0[0, :] = 1.0
_BDIAG = np.zeros((128, 128), dtype=np.float32)
_BDIAG[0:64, 0:64] = 1.0
_BDIAG[64:128, 64:128] = 1.0
_BDIAG = _BDIAG.astype(ml_dtypes.bfloat16)
_SEL = np.zeros((128, E * 128), dtype=np.float32)
for _e in range(E):
    _SEL[_e, _e * 128:(_e + 1) * 128] = 1.0


def _build_bias(bs1, bs2, be1):
    bias = np.zeros((128, NBCOL), dtype=np.float32)
    bias[:, BC_EPS] = EPS_LN
    bias[:, BC_BS1:BC_BS1 + 16] = bs1[:HS].reshape(16, 128).T
    bias[:, BC_BS1 + 16:BC_BS1 + 32] = bs1[HS:].reshape(16, 128).T
    bias[:, BC_BS2:BC_BS2 + 8] = bs2.reshape(8, 128).T
    for e in range(E):
        bias[:, BC_BE1 + e * 48:BC_BE1 + e * 48 + 24] = \
            be1[e, :HR].reshape(24, 128).T
        bias[:, BC_BE1 + e * 48 + 24:BC_BE1 + e * 48 + 48] = \
            be1[e, HR:].reshape(24, 128).T
    return bias


def kernel(L0_attn_x, L0_attn_Wq, L0_attn_Wk, L0_attn_Wv, L0_attn_Wo,
           L0_ffn_Ws1, L0_ffn_bs1, L0_ffn_Ws2, L0_ffn_bs2, L0_ffn_We1,
           L0_ffn_be1, L0_ffn_We2, L0_ffn_be2, L0_ffn_Wr, L0_ffn_br):
    if "nc" not in _CACHED:
        _CACHED["nc"] = build_program()
    nc = _CACHED["nc"]

    x = np.asarray(L0_attn_x, dtype=np.float32)
    shared = {
        "wqT": np.ascontiguousarray(np.asarray(L0_attn_Wq, np.float32).T),
        "wkT": np.ascontiguousarray(np.asarray(L0_attn_Wk, np.float32).T),
        "wvT": np.ascontiguousarray(np.asarray(L0_attn_Wv, np.float32).T),
        "woT": np.ascontiguousarray(np.asarray(L0_attn_Wo, np.float32).T),
        "ws1": np.ascontiguousarray(np.asarray(L0_ffn_Ws1, np.float32)),
        "ws2": np.ascontiguousarray(np.asarray(L0_ffn_Ws2, np.float32)),
        "we1": np.ascontiguousarray(np.asarray(L0_ffn_We1, np.float32)),
        "we2": np.ascontiguousarray(np.asarray(L0_ffn_We2, np.float32)),
        "wr": np.ascontiguousarray(np.asarray(L0_ffn_Wr, np.float32)),
        "be2": np.ascontiguousarray(np.asarray(L0_ffn_be2, np.float32)),
        "bias": _build_bias(np.asarray(L0_ffn_bs1, np.float32),
                            np.asarray(L0_ffn_bs2, np.float32),
                            np.asarray(L0_ffn_be1, np.float32)),
        "brb": np.ascontiguousarray(
            np.broadcast_to(np.asarray(L0_ffn_br, np.float32)[None, :],
                            (128, E))),
        "maskT": np.triu(np.ones((CH, CH), np.float32)).astype(ml_dtypes.bfloat16),
        "bdiag": _BDIAG,
        "idb": np.eye(128, dtype=np.float32).astype(ml_dtypes.bfloat16),
        "idf": np.eye(128, dtype=np.float32),
        "onesT": np.ones((128, 128), dtype=np.float32),
        "sel0": _SEL0,
        "sel": _SEL,
        "onebf": np.ones((128, 1), np.float32).astype(ml_dtypes.bfloat16),
    }
    in_maps = []
    for i in range(8):
        m = dict(shared)
        m["xT"] = np.ascontiguousarray(x[i].T)
        in_maps.append(m)

    res = run_bass_kernel_spmd(nc, in_maps, core_ids=list(range(8)))
    outs = [np.ascontiguousarray(r["out"].T) for r in res.results]
    return np.stack(outs, axis=0)


# version 07
# speedup vs baseline: 1.1687x; 1.1687x over previous
import numpy as np
import ml_dtypes
import concourse.bass as bass
import concourse.mybir as mybir
from concourse.bass_utils import run_bass_kernel_spmd

F32 = mybir.dt.float32
BF16 = mybir.dt.bfloat16
ALU = mybir.AluOpType
AF = mybir.ActivationFunctionType
AX = mybir.AxisListType

T = 2048
D = 1024
NF = 8
NH = 16
HD = 64
CH = 128
QT = 512
NQ = T // QT
CPQ = QT // CH
HT = 1024
NS = 512
E = 12
HS = 2048
HR = 3072
TAU = 1.5
EPS_LN = 1e-5
EPS_AT = 1e-12

BC_EPS = 0
BC_BS1 = 1
BC_BS2 = 33
BC_BE1 = 41
NBCOL = 41 + E * 48

PHASE_LIMIT = 2
ATTN_STAGE = 3
NQL = NQ
NKT = 32


class _Eng:
    def __init__(self, name, step):
        self.name = name
        self.step = step
        self.ops = []
        self.sem = None


class _Op:
    __slots__ = ("eng", "fn", "inc", "value")

    def __init__(self, eng, fn):
        self.eng = eng
        self.fn = fn
        self.inc = False
        self.value = None


class Prog:
    def __init__(self):
        self.pe = _Eng("pe", 1)
        self.ve = _Eng("ve", 1)
        self.ac = _Eng("ac", 1)
        self.gp = _Eng("gp", 1)
        self.sy = _Eng("sy", 1)
        self.wds = [_Eng(f"wd{i}", 16) for i in range(12)]
        self.xds = [_Eng(f"xds{i}", 16) for i in range(2)]
        self.wdx = _Eng("wdx", 16)
        self.xd = _Eng("xd", 16)
        self.st = _Eng("st", 16)
        self.issue = [self.pe, self.ve, self.ac, self.gp, self.sy]
        self.last_dma = {}

    def op(self, eng, fn, deps=()):
        for d in deps:
            self.wait(eng, d)
        o = _Op(eng, fn)
        eng.ops.append(o)
        return o

    def dma(self, issue_eng, done_eng, fn, deps=()):
        for d in deps:
            self.wait(issue_eng, d)
        o = _Op(done_eng, fn)
        o.inc = True
        issue_eng.ops.append(o)
        self.last_dma[id(done_eng)] = o
        return o

    def wait(self, eng, opref):
        if opref is None:
            return
        if opref.eng in (self.xd, self.st, self.wdx):
            opref = self.last_dma[id(opref.eng)]
        opref.inc = True
        eng.ops.append(("w", opref))

    def barrier(self, engines):
        lasts = []
        for e in engines:
            for it in reversed(e.ops):
                if isinstance(it, _Op):
                    lasts.append(it)
                    break
        for e in engines:
            for l in lasts:
                self.wait(e, l)

    def finalize(self):
        counts = {}
        for e in self.issue:
            for it in e.ops:
                if isinstance(it, _Op) and it.inc:
                    k = id(it.eng)
                    counts[k] = counts.get(k, 0) + 1
                    it.value = counts[k] * it.eng.step

    def replay(self, e, iface):
        last_wait = {}
        for it in e.ops:
            if isinstance(it, _Op):
                instr = it.fn(iface)
                if it.inc:
                    instr.then_inc(it.eng.sem, it.eng.step)
            else:
                _, ref = it
                assert ref.value is not None
                k = id(ref.eng)
                if last_wait.get(k, -1) >= ref.value:
                    continue
                iface.wait_ge(ref.eng.sem, ref.value)
                last_wait[k] = ref.value


def build_program():
    nc = bass.Bass("TRN2", target_bir_lowering=False)
    p = Prog()

    xT = nc.dram_tensor("xT", [D, T], F32, kind="ExternalInput").ap()
    wqT = nc.dram_tensor("wqT", [D, D], F32, kind="ExternalInput").ap()
    wkT = nc.dram_tensor("wkT", [D, D], F32, kind="ExternalInput").ap()
    wvT = nc.dram_tensor("wvT", [D, D], F32, kind="ExternalInput").ap()
    woT = nc.dram_tensor("woT", [D, D], F32, kind="ExternalInput").ap()
    ws1 = nc.dram_tensor("ws1", [D, 2 * HS], F32, kind="ExternalInput").ap()
    ws2 = nc.dram_tensor("ws2", [HS, D], F32, kind="ExternalInput").ap()
    we1 = nc.dram_tensor("we1", [E, D, 2 * HR], F32, kind="ExternalInput").ap()
    we2 = nc.dram_tensor("we2", [E, HR, D], F32, kind="ExternalInput").ap()
    wr = nc.dram_tensor("wr", [D, E], F32, kind="ExternalInput").ap()
    be2d = nc.dram_tensor("be2", [E, D], F32, kind="ExternalInput").ap()
    bias_in = nc.dram_tensor("bias", [128, NBCOL], F32, kind="ExternalInput").ap()
    brb_in = nc.dram_tensor("brb", [128, E], F32, kind="ExternalInput").ap()
    maskT_in = nc.dram_tensor("maskT", [CH, CH], BF16, kind="ExternalInput").ap()
    bdiag_in = nc.dram_tensor("bdiag", [CH, CH], BF16, kind="ExternalInput").ap()
    idb_in = nc.dram_tensor("idb", [128, 128], BF16, kind="ExternalInput").ap()
    idf_in = nc.dram_tensor("idf", [128, 128], F32, kind="ExternalInput").ap()
    onesT_in = nc.dram_tensor("onesT", [128, 128], F32, kind="ExternalInput").ap()
    sel0_in = nc.dram_tensor("sel0", [128, 128], F32, kind="ExternalInput").ap()
    sel_in = nc.dram_tensor("sel", [128, E * 128], F32, kind="ExternalInput").ap()
    oneb_in = nc.dram_tensor("onebf", [128, 1], BF16, kind="ExternalInput").ap()
    out = nc.dram_tensor("out", [D, T], F32, kind="ExternalOutput").ap()
    x1s = nc.dram_tensor("x1s", [D, T], F32, kind="Internal").ap()

    we1f = we1.rearrange("e r m -> (e r) m")
    we2f = we2.rearrange("e r d -> (e r) d")

    ctx = []

    def sb(name, shape, dtype):
        cm = nc.sbuf_tensor(name, shape, dtype)
        h = cm.__enter__()
        ctx.append(cm)
        return h

    def psb(name, shape, dtype=F32):
        cm = nc.psum_tensor(name, shape, dtype)
        h = cm.__enter__()
        ctx.append(cm)
        return h

    class scope:
        def __enter__(self):
            self.mark = len(ctx)
            return self

        def __exit__(self, *a):
            while len(ctx) > self.mark:
                ctx.pop().__exit__(None, None, None)
            return False

    c_mask = sb("c_mask", [CH, CH], BF16)
    c_bdiag = sb("c_bdiag", [CH, CH], BF16)
    c_idb = sb("c_idb", [128, 128], BF16)
    c_idf = sb("c_idf", [128, 128], F32)
    c_onesT = sb("c_onesT", [128, 128], F32)
    c_oneb = sb("c_oneb", [128, 1], BF16)
    c_sel0 = sb("c_sel0", [128, 128], F32)
    c_sel = sb("c_sel", [128, E * 128], F32)
    c_rows = sb("c_rows", [128, 3 * NS], F32)
    c_bias = sb("c_bias", [128, NBCOL], F32)
    c_brb = sb("c_brb", [128, E], F32)
    c_be2s = sb("c_be2s", [128, D], F32)
    c_wrb = sb("c_wrb", [128, NF * E], BF16)
    wring = sb("wring", [128, 12 * 1024], BF16)
    g_psA = psb("g_psA", [128, NS])
    g_psB = psb("g_psB", [128, NS])
    g_psC = psb("g_psC", [128, NS])
    g_psD = psb("g_psD", [128, NS])
    g_psE = psb("g_psE", [128, NS])
    g_psF = psb("g_psF", [128, NS])
    g_psG = psb("g_psG", [128, NS])
    g_psT = psb("g_psT", [128, 1024], BF16)

    eps_b = c_bias[:, BC_EPS:BC_EPS + 1]

    tk_c = []
    for dst, src in ((c_mask, maskT_in), (c_bdiag, bdiag_in), (c_idb, idb_in),
                     (c_idf, idf_in),
                     (c_onesT, onesT_in), (c_sel0, sel0_in), (c_sel, sel_in),
                     (c_bias, bias_in), (c_brb, brb_in), (c_oneb, oneb_in)):
        tk_c.append(p.dma(p.sy, p.xd,
                          lambda s, dst=dst, src=src: s.dma_start(dst[:], src[:])))
    (TK_MASK, TK_BDIAG, TK_IDB, TK_IDF, TK_ONES, TK_SEL0, TK_SEL, TK_BIAS,
     TK_BRB, TK_ONEB) = tk_c
    t_rows0 = p.op(p.gp, lambda g: g.memset(c_rows[:], 0.0))
    t_ms1 = p.op(p.gp, lambda g: g.memset(c_be2s[:], 0.0))
    t_be2 = p.dma(p.sy, p.xd, lambda s: s.dma_start(c_be2s[0:E, :], be2d[:, :]),
                  deps=[t_ms1])
    t_wrb = p.dma(p.gp, p.wdx, lambda g: g.dma_start(
        c_wrb.ap().rearrange("p (jc e) -> p jc e", jc=NF),
        wr.rearrange("(jc p) e -> p jc e", p=128)))

    NSLOT = 12
    slot_free = [None] * NSLOT
    ring_state = [0]

    def wload(dram_2d, c0, c1, njc, nslots, r0=0):
        pptr = ring_state[0]
        if pptr + nslots > NSLOT:
            pptr = 0
        ring_state[0] = (pptr + nslots) % NSLOT if pptr + nslots >= NSLOT else pptr + nslots
        deps = [slot_free[pptr + i] for i in range(nslots)]
        off = pptr * 1024
        ncols = c1 - c0
        dst = wring[:, off:off + njc * ncols].rearrange("p (jc c) -> p jc c", jc=njc)
        src = dram_2d[r0:r0 + njc * 128, c0:c1].rearrange("(jc p) c -> p jc c", p=128)
        tok = p.dma(p.gp, p.wds[pptr], lambda g, dst=dst, src=src: g.dma_start(dst, src),
                    deps=[d for d in deps if d is not None])
        return off, tok, pptr, nslots

    def wrelease(pptr, nslots, tok):
        for i in range(nslots):
            slot_free[pptr + i] = tok

    def emit_ln(src_fn, src_tok, ps4, sqpp, extra=None):
        rows = c_rows
        ps_sum, ps_sq, ps_sc, ps_b = ps4
        src_tok = src_tok if isinstance(src_tok, list) else [src_tok]
        sq_toks = [None] * NF
        mm_sq = [None] * NF
        last_sum = None
        for fc in range(NF):
            deps = src_tok + [TK_ONES]
            if fc == 0 and extra is not None:
                deps.append(extra)
            last_sum = p.op(p.pe, lambda t, fc=fc: t.matmul(
                ps_sum[:, :], c_onesT[:, :], src_fn(fc),
                start=(fc == 0), stop=(fc == NF - 1)), deps=deps)
        for fc in range(NF):
            dsq = list(src_tok)
            if fc < 2 and extra is not None:
                dsq.append(extra)
            if fc >= 2:
                dsq.append(mm_sq[fc - 2])
            sq_toks[fc] = p.op(p.ac, lambda a, fc=fc: a.activation(
                sqpp[:, (fc % 2) * NS:(fc % 2) * NS + NS], src_fn(fc), AF.Square),
                deps=dsq)
            dmm = [sq_toks[fc]]
            if fc == 0 and extra is not None:
                dmm.append(extra)
            mm_sq[fc] = p.op(p.pe, lambda t, fc=fc: t.matmul(
                ps_sq[:, :], c_onesT[:, :], sqpp[:, (fc % 2) * NS:(fc % 2) * NS + NS],
                start=(fc == 0), stop=(fc == NF - 1)), deps=dmm)
        t_sum, t_sq = last_sum, mm_sq[-1]
        ex = [extra] if extra is not None else []
        s1 = p.op(p.ve, lambda v: v.tensor_scalar(
            rows[0:1, 2 * NS:3 * NS], ps_sum[0:1, :], 1.0 / D, None,
            ALU.mult, ALU.bypass), deps=[t_sum] + ex)
        s2 = p.op(p.ve, lambda v: v.tensor_tensor(
            rows[0:1, 0:NS], rows[0:1, 2 * NS:3 * NS],
            rows[0:1, 2 * NS:3 * NS], ALU.mult), deps=[s1] + ex)
        s3 = p.op(p.ve, lambda v: v.tensor_scalar(
            ps_sc[0:1, :], ps_sq[0:1, :], 1.0 / D, None, ALU.mult, ALU.bypass),
            deps=[t_sq] + ex)
        s4 = p.op(p.ve, lambda v: v.tensor_tensor(
            ps_sq[0:1, :], ps_sc[0:1, :], rows[0:1, 0:NS], ALU.subtract),
            deps=[s2, s3])
        s5 = p.op(p.ac, lambda a: a.activation(
            ps_sc[0:1, :], ps_sq[0:1, :], AF.Sqrt, bias=eps_b[0:1, :]),
            deps=[s4, TK_BIAS])
        s6 = p.op(p.ve, lambda v: v.reciprocal(rows[0:1, 0:NS], ps_sc[0:1, :]),
                  deps=[s5, s4])
        s7 = p.op(p.ve, lambda v: v.tensor_tensor(
            rows[0:1, NS:2 * NS], rows[0:1, 2 * NS:3 * NS], rows[0:1, 0:NS],
            ALU.mult), deps=[s6] + ex)
        s8 = p.op(p.ve, lambda v: v.tensor_scalar(
            rows[0:1, NS:2 * NS], rows[0:1, NS:2 * NS], -1.0, None,
            ALU.mult, ALU.bypass), deps=[s7])
        t_rb = p.op(p.pe, lambda t: t.matmul(
            ps_sum[:, :], c_sel0[:, :], rows[:, 0:NS], start=True, stop=True),
            deps=[s6, TK_SEL0, t_rows0])
        t_bb = p.op(p.pe, lambda t: t.matmul(
            ps_b[:, :], c_sel0[:, :], rows[:, NS:2 * NS], start=True,
            stop=True), deps=[s8, s3])
        return t_rb, t_bb

    attn_scope = scope()
    attn_scope.__enter__()
    Sbf = sb("Sbf", [128, NF * 130], BF16)
    Sf32 = sb("Sf32", [128, NF * 128], F32)
    zf32 = sb("zf32", [128, NF], F32)
    zbf = sb("zbf", [128, NF], BF16)
    x0bufs = [sb("x0a", [128, NF * QT], F32), sb("x0b", [128, NF * QT], F32)]
    t_szero = p.op(p.gp, lambda g: g.memset(Sbf[:], 0.0))
    t_s32z = p.op(p.gp, lambda g: g.memset(Sf32[:], 0.0))
    t_zz = p.op(p.gp, lambda g: g.memset(zf32[:], 0.0))
    t_x0_tok = {}
    t_x0_tok[0] = [
        p.dma(p.sy, p.xds[hh], lambda s, hh=hh: s.dma_start(
            x0bufs[0].ap().rearrange("p (fc t) -> p fc t", fc=NF)[:, hh * 4:hh * 4 + 4, :],
            xT.rearrange("(fc p) t -> p fc t", p=128)[:, hh * 4:hh * 4 + 4, 0:QT]))
        for hh in range(2)]
    if NQL > 1:
        t_x0_tok[1] = [p.dma(p.sy, p.xds[1], lambda s: s.dma_start(
            x0bufs[1].ap().rearrange("p (fc t) -> p fc t", fc=NF),
            xT.rearrange("(fc p) t -> p fc t", p=128)[:, :, QT:2 * QT]))]

    t_sbf_prev = {}
    t_zadd = {}
    t_c1prev = {ic: None for ic in range(NF)}
    t_zc_prev = [None]
    st_x1 = []

    def _do_quarter(q):
        qsl = slice(q * QT, (q + 1) * QT)
        with scope():
            x0q = x0bufs[q % 2]
            sqpp = sb(f"sqpp{q}", [128, 2 * NS], F32)
            xnq = sb(f"xnq{q}", [128, NF * QT], BF16)
            qTq = sb(f"qTq{q}", [128, NF * QT], BF16)
            kTq = sb(f"kTq{q}", [128, NF * QT], BF16)
            vts = sb(f"vts{q}", [128, CPQ * D], BF16)
            kts = sb(f"kts{q}", [128, CPQ * D], BF16)
            eltmp = sb(f"eltmp{q}", [128, 2 * NS], F32)
            eltm2 = sb(f"eltm2{q}", [128, 2 * NS], F32)
            scs = sb(f"scs{q}", [128, 2 * CH], BF16)
            rcpq = sb(f"rcpq{q}", [128, 2], F32)
            ptok = sb(f"ptok{q}", [128, 2 * D], BF16)
            attnT = sb(f"attnT{q}", [128, NF * QT], BF16)
            yq = sb(f"yq{q}", [128, NF * QT], F32)
            psA, psB, psC, psD, psE, psF, psG = (
                g_psA, g_psB, g_psC, g_psD, g_psE, g_psF, g_psG)
            ptr = g_psT

            p.barrier([p.pe, p.ve, p.ac, p.sy])
            t_x0 = t_x0_tok[q]

            t_rb, t_bb = emit_ln(
                lambda fc: x0q[:, fc * QT:fc * QT + NS], t_x0,
                (psC, psD, psG, psD), sqpp)
            t_xn = []
            for fc in range(NF):
                sl = slice((fc % 2) * NS, (fc % 2) * NS + NS)
                ta = p.op(p.ve, lambda v, fc=fc, sl=sl: v.tensor_tensor(
                    eltmp[:, sl], x0q[:, fc * QT:fc * QT + NS], psC[:, :],
                    ALU.mult), deps=[t_rb])
                tb = p.op(p.ve, lambda v, fc=fc, sl=sl: v.tensor_tensor(
                    xnq[:, fc * QT:fc * QT + NS], eltmp[:, sl], psD[:, :],
                    ALU.add), deps=[ta, t_bb])
                t_xn.append(tb)
            t_xn_all = t_xn[-1]

            def featproj_elu(wT_dram, dstT, prev_bank):
                wins = [wload(wT_dram, w * 128, (w + 1) * 128, NF, 1)
                        for w in range(NF)]
                evac = []
                for ic in range(NF):
                    off, wtok, pp, nsl = wins[ic]
                    bank = psC if ic % 2 == 0 else psD
                    deps = [wtok, t_xn_all]
                    if ic >= 2:
                        deps.append(evac[ic - 2])
                    elif prev_bank[ic % 2] is not None:
                        deps.append(prev_bank[ic % 2])
                    mm = None
                    for jc in range(NF):
                        mm = p.op(p.pe, lambda t, off=off, jc=jc, bank=bank: t.matmul(
                            bank[:, :],
                            wring[:, off + jc * 128:off + (jc + 1) * 128],
                            xnq[:, jc * QT:jc * QT + NS],
                            start=(jc == 0), stop=(jc == NF - 1)),
                            deps=deps if jc == 0 else ())
                    wrelease(pp, nsl, mm)
                    sl = slice((ic % 2) * NS, (ic % 2) * NS + NS)
                    dte = [mm]
                    if ic >= 2:
                        dte.append(evac[ic - 2])
                    te = p.op(p.ac, lambda a, bank=bank, sl=sl: a.activation(
                        eltmp[:, sl], bank[:, :], AF.Exp), deps=dte)
                    tr2 = p.op(p.ac, lambda a, bank=bank, sl=sl: a.activation(
                        eltm2[:, sl], bank[:, :], AF.Relu), deps=[te])
                    tm = p.op(p.ve, lambda v, sl=sl: v.tensor_scalar(
                        eltmp[:, sl], eltmp[:, sl], 1.0, None, ALU.min, ALU.bypass),
                        deps=[te])
                    ta2 = p.op(p.ve, lambda v, ic=ic, sl=sl, dstT=dstT: v.tensor_tensor(
                        dstT[:, ic * QT:ic * QT + NS], eltmp[:, sl], eltm2[:, sl],
                        ALU.add), deps=[tm, tr2])
                    evac.append(ta2)
                return evac

            tq_toks = featproj_elu(wqT, qTq, (None, None))
            tk_toks = featproj_elu(wkT, kTq, (tq_toks[-2], tq_toks[-1]))

            vwins = [wload(wvT, w * 128, (w + 1) * 128, NF, 1) for w in range(NF)]
            v_toks = []
            for c8 in range(CPQ):
                for ni in range(2):
                    it = c8 * 2 + ni
                    bank = psC if it % 2 == 0 else psD
                    deps = [t_xn_all]
                    if it >= 2:
                        deps.append(v_toks[it - 2])
                    else:
                        deps.append(tk_toks[NF - 2 + it])
                    mm = None
                    nmm = 0
                    for wi in range(4 * ni, 4 * ni + 4):
                        off, wtok, pp, nsl = vwins[wi]
                        d2 = list(deps) + [wtok] if nmm == 0 else [wtok]
                        for jc in range(NF):
                            st_ = (nmm == 0)
                            sp_ = (nmm == 31)
                            mm = p.op(p.pe, lambda t, off=off, jc=jc, wi=wi, ni=ni,
                                      c8=c8, bank=bank, st_=st_, sp_=sp_: t.matmul(
                                bank[:, (wi - 4 * ni) * 128:(wi - 4 * ni + 1) * 128],
                                xnq[:, jc * QT + c8 * 128:jc * QT + (c8 + 1) * 128],
                                wring[:, off + jc * 128:off + (jc + 1) * 128],
                                start=st_, stop=sp_),
                                deps=d2 if jc == 0 else ())
                            nmm += 1
                    tv = p.op(p.ac, lambda a, c8=c8, ni=ni, bank=bank: a.activation(
                        vts[:, c8 * D + ni * NS:c8 * D + ni * NS + NS], bank[:, :],
                        AF.Copy), deps=[mm])
                    v_toks.append(tv)
                    if c8 == CPQ - 1:
                        v_last_mm = mm
            for wi in range(NF):
                off, wtok, pp, nsl = vwins[wi]
                wrelease(pp, nsl, v_last_mm)
            vall = v_toks[-1]

            kt_toks = []
            kt_prev_ev = None
            for c8 in range(CPQ):
                for icp in range(0, NF, 2):
                    deps = [tk_toks[icp], tk_toks[icp + 1], TK_IDB]
                    if kt_prev_ev is not None:
                        deps.append(kt_prev_ev)
                    p.op(p.pe, lambda t, c8=c8, ic=icp: t.transpose(
                        ptr[:, 0:128],
                        kTq[:, ic * QT + c8 * 128:ic * QT + (c8 + 1) * 128],
                        c_idb[:, :]), deps=deps)
                    tt = p.op(p.pe, lambda t, c8=c8, ic=icp + 1: t.transpose(
                        ptr[:, 128:256],
                        kTq[:, ic * QT + c8 * 128:ic * QT + (c8 + 1) * 128],
                        c_idb[:, :]))
                    tc = p.op(p.ve, lambda v, c8=c8, ic=icp: v.tensor_copy(
                        kts[:, c8 * D + ic * 128:c8 * D + (ic + 2) * 128],
                        ptr[:, 0:256]), deps=[tt])
                    kt_prev_ev = tc
                    kt_toks.extend([tc, tc])

            t_sc_ev = {}
            t_nd_pe = {}
            t_div = {}
            t_at_ev = {}
            t_tok_tr = {}
            t_div_last = None
            for c8 in range(CPQ):
                cglob = q * CPQ + c8
                for ic in range(NF):
                    key = (c8, ic)
                    ndb = psC if ic % 2 == 0 else psD
                    qc = slice(ic * QT + c8 * 128, ic * QT + (c8 + 1) * 128)
                    s0 = ic * 130
                    pk = (c8, ic - 2) if ic >= 2 else \
                        ((c8 - 1, NF - 2 + ic) if c8 > 0 else None)
                    pk1 = (c8, ic - 1) if ic >= 1 else \
                        ((c8 - 1, NF - 1) if c8 > 0 else None)

                    deps = [tq_toks[ic], tk_toks[ic]]
                    if pk1 is not None:
                        deps.append(t_sc_ev[pk1])
                    t_s0 = p.op(p.pe, lambda t, qc=qc: t.matmul(
                        psE[:, 0:128], kTq[0:64, qc], qTq[0:64, qc],
                        start=True, stop=True), deps=deps)
                    t_s1 = p.op(p.pe, lambda t, qc=qc: t.matmul(
                        psF[:, 0:128], kTq[64:128, qc], qTq[64:128, qc],
                        start=True, stop=True), deps=[t_s0])

                    ev0 = p.op(p.ve, lambda v: v.tensor_tensor(
                        scs[:, 0:CH], psE[:, 0:128], c_mask[:, :], ALU.mult),
                        deps=[t_s1, TK_MASK])
                    ev1 = p.op(p.ve, lambda v: v.tensor_tensor(
                        scs[:, CH:2 * CH], psF[:, 0:128], c_mask[:, :], ALU.mult),
                        deps=[ev0])
                    t_sc_ev[key] = ev1

                    use_sbf = cglob > 0
                    deps_nd = [ev1, vall, TK_ONEB]
                    if use_sbf:
                        deps_nd += t_sbf_prev[ic]
                    if pk is not None:
                        deps_nd.append(t_div[pk])
                    else:
                        deps_nd.append(v_toks[6 + (ic % 2)])
                    mm = None
                    imm = 0
                    nmm_tot = 8 if use_sbf else 4
                    for hh in range(2):
                        hsl = slice(hh * 64, (hh + 1) * 64)
                        if use_sbf:
                            st_ = (imm == 0)
                            mm = p.op(p.pe, lambda t, qc=qc, hsl=hsl, ndb=ndb,
                                      s0=s0, hh=hh, st_=st_: t.matmul(
                                ndb[:, hsl], qTq[:, qc],
                                Sbf[:, s0 + hh * 64:s0 + (hh + 1) * 64],
                                start=st_, stop=False),
                                deps=deps_nd if imm == 0 else ())
                            imm += 1
                            mm = p.op(p.pe, lambda t, qc=qc, ndb=ndb, s0=s0,
                                      hh=hh: t.matmul(
                                ndb[:, 128 + hh:129 + hh], qTq[:, qc],
                                Sbf[:, s0 + 128 + hh:s0 + 129 + hh],
                                start=False, stop=False))
                            imm += 1
                        st_ = (imm == 0)
                        mm = p.op(p.pe, lambda t, hh=hh, hsl=hsl, ndb=ndb,
                                  c8=c8, ic=ic, st_=st_: t.matmul(
                            ndb[:, hsl], scs[:, hh * CH:(hh + 1) * CH],
                            vts[:, c8 * D + ic * 128 + hh * 64:
                                 c8 * D + ic * 128 + hh * 64 + 64],
                            start=st_, stop=False),
                            deps=deps_nd if imm == 0 else ())
                        imm += 1
                        sp_ = (hh == 1)
                        mm = p.op(p.pe, lambda t, hh=hh, ndb=ndb, sp_=sp_: t.matmul(
                            ndb[:, 128 + hh:129 + hh], scs[:, hh * CH:(hh + 1) * CH],
                            c_oneb[:, :], start=False, stop=sp_))
                        imm += 1
                    t_nd = mm
                    t_nd_pe[key] = t_nd

                    r1 = p.op(p.ve, lambda v, ndb=ndb: v.reciprocal(
                        rcpq[:, 0:2], ndb[:, 128:130]),
                        deps=[t_nd, t_div_last])
                    dv0 = p.op(p.ve, lambda v, ndb=ndb, c8=c8, ic=ic: v.tensor_scalar(
                        ptok[:, (c8 % 2) * D + ic * 128:(c8 % 2) * D + ic * 128 + 64],
                        ndb[:, 0:64], rcpq[:, 0:1], EPS_AT, ALU.mult, ALU.add),
                        deps=[r1, t_tok_tr.get((c8 - 2, ic))])
                    dv1 = p.op(p.ve, lambda v, ndb=ndb, c8=c8, ic=ic: v.tensor_scalar(
                        ptok[:, (c8 % 2) * D + ic * 128 + 64:
                             (c8 % 2) * D + (ic + 1) * 128],
                        ndb[:, 64:128], rcpq[:, 1:2], EPS_AT, ALU.mult, ALU.add),
                        deps=[dv0])
                    t_div[key] = dv1
                    t_div_last = dv1

                for ic in range(NF):
                    dsb = psA if ic % 2 == 0 else psB
                    vc = slice(c8 * D + ic * 128, c8 * D + (ic + 1) * 128)
                    s0 = ic * 130
                    f0 = ic * 128
                    key = (c8, ic)
                    pk = (c8, ic - 2) if ic >= 2 else \
                        ((c8 - 1, NF - 2 + ic) if c8 > 0 else None)
                    deps_ds = [kt_toks[c8 * NF + ic], vall]
                    if pk is not None:
                        deps_ds.append(t_zadd.get(pk))
                    m1_ = p.op(p.pe, lambda t, dsb=dsb, vc=vc: t.matmul(
                        dsb[:, 0:128], kts[:, vc], vts[:, vc],
                        start=True, stop=False), deps=deps_ds)
                    t_ds = p.op(p.pe, lambda t, dsb=dsb, vc=vc: t.matmul(
                        dsb[:, 128:129], kts[:, vc], c_oneb[:, :],
                        start=False, stop=True))
                    da = [t_ds]
                    if t_c1prev[ic] is not None:
                        da.append(t_c1prev[ic])
                    else:
                        da += [t_s32z, t_zz]
                    a1 = p.op(p.ve, lambda v, dsb=dsb, f0=f0: v.tensor_tensor(
                        Sf32[0:64, f0:f0 + 64], Sf32[0:64, f0:f0 + 64],
                        dsb[0:64, 0:64], ALU.add), deps=da)
                    a2 = p.op(p.ve, lambda v, dsb=dsb, f0=f0: v.tensor_tensor(
                        Sf32[64:128, f0 + 64:f0 + 128],
                        Sf32[64:128, f0 + 64:f0 + 128],
                        dsb[64:128, 64:128], ALU.add), deps=[a1])
                    za = p.op(p.ve, lambda v, dsb=dsb, ic=ic: v.tensor_tensor(
                        zf32[:, ic:ic + 1], zf32[:, ic:ic + 1],
                        dsb[:, 128:129], ALU.add), deps=[a2])
                    dc1 = [a2, t_nd_pe[key]]
                    if cglob == 0:
                        dc1.append(t_szero)
                    c1 = p.op(p.ac, lambda a, f0=f0, s0=s0: a.activation(
                        Sbf[:, s0:s0 + 128], Sf32[:, f0:f0 + 128], AF.Copy),
                        deps=dc1)
                    z0 = p.op(p.ac, lambda a, ic=ic, s0=s0: a.activation(
                        Sbf[0:64, s0 + 128:s0 + 129], zf32[0:64, ic:ic + 1],
                        AF.Copy), deps=[za, c1])
                    z1 = p.op(p.ac, lambda a, ic=ic, s0=s0: a.activation(
                        Sbf[64:128, s0 + 129:s0 + 130], zf32[64:128, ic:ic + 1],
                        AF.Copy), deps=[z0])
                    t_zadd[key] = za
                    t_c1prev[ic] = z1
                    t_sbf_prev[ic] = [z1]

                for fcp in range(0, NF, 2):
                    j = c8 * NF + fcp
                    deps = [t_div[(c8, fcp)], t_div[(c8, fcp + 1)], TK_IDB,
                            kt_prev_ev]
                    if (c8, fcp) != (0, 0):
                        deps.append(at_prev_ev)
                    tt0 = p.op(p.pe, lambda t, c8=c8, fc=fcp: t.transpose(
                        ptr[:, 0:128],
                        ptok[:, (c8 % 2) * D + fc * 128:(c8 % 2) * D + (fc + 1) * 128],
                        c_idb[:, :]), deps=deps)
                    tt = p.op(p.pe, lambda t, c8=c8, fc=fcp + 1: t.transpose(
                        ptr[:, 128:256],
                        ptok[:, (c8 % 2) * D + fc * 128:(c8 % 2) * D + (fc + 1) * 128],
                        c_idb[:, :]))
                    tc0 = p.op(p.ve, lambda v, c8=c8, fc=fcp: v.tensor_copy(
                        attnT[:, fc * QT + c8 * 128:fc * QT + (c8 + 1) * 128],
                        ptr[:, 0:128]), deps=[tt])
                    tc = p.op(p.ve, lambda v, c8=c8, fc=fcp + 1: v.tensor_copy(
                        attnT[:, fc * QT + c8 * 128:fc * QT + (c8 + 1) * 128],
                        ptr[:, 128:256]), deps=[tc0])
                    at_prev_ev = tc
                    t_at_ev[j] = tc
                    t_at_ev[j + 1] = tc
                    t_tok_tr[(c8, fcp)] = tt0
                    t_tok_tr[(c8, fcp + 1)] = tt
            t_attnT_all = t_at_ev[CPQ * NF - 1]

            owins = [wload(woT, w * 128, (w + 1) * 128, NF, 1) for w in range(NF)]
            y_toks = []
            for ic in range(NF):
                off, wtok, pp, nsl = owins[ic]
                bank = psC if ic % 2 == 0 else psD
                deps = [wtok, t_attnT_all]
                if ic >= 2:
                    deps.append(y_toks[ic - 2])
                mm = None
                for jc in range(NF):
                    mm = p.op(p.pe, lambda t, off=off, jc=jc, bank=bank: t.matmul(
                        bank[:, :], wring[:, off + jc * 128:off + (jc + 1) * 128],
                        attnT[:, jc * QT:jc * QT + NS],
                        start=(jc == 0), stop=(jc == NF - 1)),
                        deps=deps if jc == 0 else ())
                wrelease(pp, nsl, mm)
                ty = p.op(p.ac, lambda a, ic=ic, bank=bank: a.activation(
                    yq[:, ic * QT:ic * QT + NS], bank[:, :], AF.Copy), deps=[mm])
                y_toks.append(ty)
            t_y_all = y_toks[-1]

            t_rb2, t_bb2 = emit_ln(
                lambda fc: yq[:, fc * QT:fc * QT + NS], t_y_all,
                (psC, psD, psG, psD), sqpp, extra=t_attnT_all)
            x1_toks = []
            for fc in range(NF):
                sl = slice((fc % 2) * NS, (fc % 2) * NS + NS)
                tm = p.op(p.ve, lambda v, fc=fc, sl=sl: v.tensor_tensor(
                    eltmp[:, sl], yq[:, fc * QT:fc * QT + NS], psC[:, :],
                    ALU.mult), deps=[t_rb2])
                tn = p.op(p.ve, lambda v, fc=fc, sl=sl: v.tensor_tensor(
                    eltmp[:, sl], eltmp[:, sl], psD[:, :], ALU.add),
                    deps=[tm, t_bb2])
                to = p.op(p.ve, lambda v, fc=fc, sl=sl: v.tensor_tensor(
                    x0q[:, fc * QT:fc * QT + NS], x0q[:, fc * QT:fc * QT + NS],
                    eltmp[:, sl], ALU.add), deps=[tn])
                x1_toks.append(to)
            dst = out if PHASE_LIMIT == 1 else x1s
            t_st = p.dma(p.sy, p.st, lambda s, dst=dst, qsl=qsl: s.dma_start(
                dst.rearrange("(fc p) t -> p fc t", p=128)[:, :, qsl],
                x0q.ap().rearrange("p (fc t) -> p fc t", fc=NF)),
                deps=x1_toks)
            st_x1.append(t_st)
            if q + 2 < NQL:
                t_x0_tok[q + 2] = [p.dma(p.sy, p.xds[q % 2], lambda s, q=q: s.dma_start(
                    x0bufs[q % 2].ap().rearrange("p (fc t) -> p fc t", fc=NF),
                    xT.rearrange("(fc p) t -> p fc t", p=128)[:, :, (q + 2) * QT:(q + 3) * QT]),
                    deps=[t_st])]

    for q in range(NQL):
        _do_quarter(q)
    attn_scope.__exit__(None, None, None)

    def _do_half(hf):
        hsl = slice(hf * HT, (hf + 1) * HT)
        with scope():
            x1h = sb(f"x1h{hf}", [128, NF * HT], F32)
            sqpp = sb(f"fsq{hf}", [128, 2 * NS], F32)
            xn2 = sb(f"xn2{hf}", [128, NF * HT], BF16)
            yr = sb(f"yr{hf}", [128, NF * HT], F32)
            act = sb(f"act{hf}", [128, 24 * HT], BF16)
            gact = sb(f"gact{hf}", [128, 2 * HT], BF16)
            aval = sb(f"aval{hf}", [128, 2 * HT], BF16)
            wTt = sb(f"wT{hf}", [128, HT], F32)
            wtok = sb(f"wtok{hf}", [128, 128], F32)
            wb = sb(f"wb{hf}", [128, HT], F32)
            rtr = sb(f"rtr{hf}", [128, 160], F32)
            tsc = sb(f"tsc{hf}", [128, 2 * NS], F32)
            psA, psB, psC, psD, psG = g_psA, g_psB, g_psC, g_psD, g_psG
            ptrf = g_psT.ap().bitcast(F32)

            p.barrier([p.pe, p.ve, p.ac, p.sy])
            t_wz = p.op(p.gp, lambda g: g.memset(wtok[:], 0.0))

            t_x1 = p.dma(p.sy, p.xds[hf % 2], lambda s, hsl=hsl: s.dma_start(
                x1h.ap().rearrange("p (fc t) -> p fc t", fc=NF),
                x1s.rearrange("(fc p) t -> p fc t", p=128)[:, :, hsl]),
                deps=[st_x1[2 * hf], st_x1[2 * hf + 1]])

            t_xn2 = []
            prev_norm = None
            for ns_i in range(2):
                t_rb3, t_bb3 = emit_ln(
                    lambda fc, ns_i=ns_i: x1h[:, fc * HT + ns_i * NS:
                                              fc * HT + ns_i * NS + NS],
                    t_x1, (psA, psB, psC, psD), sqpp, extra=prev_norm)
                for fc in range(NF):
                    sl = slice((fc % 2) * NS, (fc % 2) * NS + NS)
                    ta = p.op(p.ve, lambda v, fc=fc, ns_i=ns_i, sl=sl: v.tensor_tensor(
                        tsc[:, sl],
                        x1h[:, fc * HT + ns_i * NS:fc * HT + ns_i * NS + NS],
                        psA[:, :], ALU.mult), deps=[t_rb3])
                    tb = p.op(p.ve, lambda v, fc=fc, ns_i=ns_i, sl=sl: v.tensor_tensor(
                        xn2[:, fc * HT + ns_i * NS:fc * HT + ns_i * NS + NS],
                        tsc[:, sl], psD[:, :], ALU.add), deps=[ta, t_bb3])
                    t_xn2.append(tb)
                prev_norm = t_xn2[-1]
            t_xn2_all = t_xn2[-1]

            t_wT = []
            t_wtr = None
            t_rtr_prev = None
            for c8 in range(NF):
                mm = None
                for jc in range(NF):
                    d0 = [t_xn2_all, t_wrb]
                    if t_rtr_prev is not None:
                        d0.append(t_rtr_prev)
                    mm = p.op(p.pe, lambda t, c8=c8, jc=jc: t.matmul(
                        psG[:, 0:E],
                        xn2[:, jc * HT + c8 * 128:jc * HT + (c8 + 1) * 128],
                        c_wrb[:, jc * E:(jc + 1) * E],
                        start=(jc == 0), stop=(jc == NF - 1)),
                        deps=d0 if jc == 0 else ())
                lg = p.op(p.ve, lambda v: v.tensor_tensor(
                    rtr[:, 0:E], psG[:, 0:E], c_brb[:, :], ALU.add),
                    deps=[mm, TK_BRB, t_rtr_prev])
                m1 = p.op(p.ve, lambda v: v.tensor_reduce(
                    rtr[:, 32:33], rtr[:, 0:E], AX.X, ALU.max), deps=[lg])
                k1 = p.op(p.ve, lambda v: v.tensor_scalar(
                    rtr[:, 48:48 + E], rtr[:, 0:E], rtr[:, 32:33], None,
                    ALU.is_equal, ALU.bypass), deps=[m1])
                k2 = p.op(p.ve, lambda v: v.tensor_scalar(
                    rtr[:, 64:64 + E], rtr[:, 48:48 + E], -1e30, None,
                    ALU.mult, ALU.bypass), deps=[k1])
                l2 = p.op(p.ve, lambda v: v.tensor_tensor(
                    rtr[:, 64:64 + E], rtr[:, 0:E], rtr[:, 64:64 + E],
                    ALU.add), deps=[k2])
                m2 = p.op(p.ve, lambda v: v.tensor_reduce(
                    rtr[:, 33:34], rtr[:, 64:64 + E], AX.X, ALU.max),
                    deps=[l2])
                k3 = p.op(p.ve, lambda v: v.tensor_scalar(
                    rtr[:, 80:80 + E], rtr[:, 64:64 + E], rtr[:, 33:34], None,
                    ALU.is_equal, ALU.bypass), deps=[m2])
                dd = p.op(p.ve, lambda v: v.tensor_tensor(
                    rtr[:, 34:35], rtr[:, 32:33], rtr[:, 33:34],
                    ALU.subtract), deps=[m2])
                w1 = p.op(p.ac, lambda a: a.activation(
                    rtr[:, 35:36], rtr[:, 34:35], AF.Sigmoid, scale=1.0 / TAU),
                    deps=[dd])
                w2 = p.op(p.ve, lambda v: v.tensor_scalar(
                    rtr[:, 36:37], rtr[:, 35:36], -1.0, 1.0, ALU.mult,
                    ALU.add), deps=[w1])
                q1 = p.op(p.ve, lambda v: v.tensor_scalar(
                    rtr[:, 96:96 + E], rtr[:, 48:48 + E], rtr[:, 35:36], None,
                    ALU.mult, ALU.bypass), deps=[w2])
                q2 = p.op(p.ve, lambda v: v.tensor_scalar(
                    rtr[:, 112:112 + E], rtr[:, 80:80 + E], rtr[:, 36:37],
                    None, ALU.mult, ALU.bypass), deps=[q1])
                wt = p.op(p.ve, lambda v: v.tensor_tensor(
                    wtok[:, 0:E], rtr[:, 96:96 + E], rtr[:, 112:112 + E],
                    ALU.add), deps=[q2, t_wz, t_wtr])
                ttr = p.op(p.pe, lambda t: t.transpose(
                    ptrf[:, 0:128], wtok[:, :], c_idf[:, :]),
                    deps=[wt, TK_IDF] + ([t_wT[-1]] if t_wT else []))
                tev = p.op(p.ve, lambda v, c8=c8: v.tensor_copy(
                    wTt[:, c8 * 128:(c8 + 1) * 128], ptrf[:, 0:128]), deps=[ttr])
                t_wtr = ttr
                t_rtr_prev = wt
                t_wT.append(tev)
            t_wT_all = t_wT[-1]

            t_acts = {}
            for sc in range(16):
                wa = wload(ws1, sc * 128, (sc + 1) * 128, NF, 1)
                wg = wload(ws1, HS + sc * 128, HS + (sc + 1) * 128, NF, 1)
                offa, toka, ppa, nsa = wa
                offg, tokg, ppg, nsg = wg
                for ns_i in range(2):
                    it = sc * 2 + ns_i
                    ba = psA if it % 2 == 0 else psC
                    bg = psB if it % 2 == 0 else psD
                    deps = [toka, t_xn2_all]
                    if it >= 2:
                        deps.append(t_acts[it - 2])
                    else:
                        deps.append(t_wT_all)
                    mma = None
                    for jc in range(NF):
                        mma = p.op(p.pe, lambda t, offa=offa, jc=jc, ba=ba,
                                   ns_i=ns_i: t.matmul(
                            ba[:, :],
                            wring[:, offa + jc * 128:offa + (jc + 1) * 128],
                            xn2[:, jc * HT + ns_i * NS:jc * HT + ns_i * NS + NS],
                            start=(jc == 0), stop=(jc == NF - 1)),
                            deps=deps if jc == 0 else ())
                    mmg = None
                    for jc in range(NF):
                        mmg = p.op(p.pe, lambda t, offg=offg, jc=jc, bg=bg,
                                   ns_i=ns_i: t.matmul(
                            bg[:, :],
                            wring[:, offg + jc * 128:offg + (jc + 1) * 128],
                            xn2[:, jc * HT + ns_i * NS:jc * HT + ns_i * NS + NS],
                            start=(jc == 0), stop=(jc == NF - 1)),
                            deps=[tokg] if jc == 0 else ())
                    ta_ = p.op(p.ac, lambda a, ba=ba, sc=sc, it=it: a.activation(
                        aval[:, (it % 2) * HT:(it % 2) * HT + NS], ba[:, :],
                        AF.Identity, bias=c_bias[:, BC_BS1 + sc:BC_BS1 + sc + 1]),
                        deps=[mma, TK_BIAS])
                    tg_ = p.op(p.ac, lambda a, bg=bg, sc=sc, it=it: a.activation(
                        gact[:, (it % 2) * HT:(it % 2) * HT + NS], bg[:, :],
                        AF.Gelu,
                        bias=c_bias[:, BC_BS1 + 16 + sc:BC_BS1 + 16 + sc + 1]),
                        deps=[mmg])
                    tm_ = p.op(p.ve, lambda v, sc=sc, ns_i=ns_i, it=it: v.tensor_tensor(
                        act[:, sc * HT + ns_i * NS:sc * HT + ns_i * NS + NS],
                        aval[:, (it % 2) * HT:(it % 2) * HT + NS],
                        gact[:, (it % 2) * HT:(it % 2) * HT + NS], ALU.mult),
                        deps=[ta_, tg_])
                    t_acts[it] = tm_
                    if ns_i == 1:
                        wrelease(ppa, nsa, mma)
                        wrelease(ppg, nsg, mmg)
            t_acts_all = t_acts[31]

            t_yr = {}
            t_bw_prev = None
            for dc in range(NF):
                w2w = wload(ws2, dc * 128, (dc + 1) * 128, 16, 2)
                off2, tok2, pp2, ns2 = w2w
                for ns_i in range(2):
                    it = dc * 2 + ns_i
                    bank = psA if it % 2 == 0 else psB
                    deps = [tok2, t_acts_all]
                    if it >= 2:
                        deps.append(t_yr[it - 2])
                    mm = None
                    for sc in range(16):
                        mm = p.op(p.pe, lambda t, off2=off2, sc=sc, bank=bank,
                                  ns_i=ns_i: t.matmul(
                            bank[:, :],
                            wring[:, off2 + sc * 128:off2 + (sc + 1) * 128],
                            act[:, sc * HT + ns_i * NS:sc * HT + ns_i * NS + NS],
                            start=(sc == 0), stop=(sc == 15)),
                            deps=deps if sc == 0 else ())
                    mw = p.op(p.pe, lambda t, dc=dc, ns_i=ns_i: t.matmul(
                        psG[:, :], c_be2s[:, dc * 128:(dc + 1) * 128],
                        wTt[:, ns_i * NS:ns_i * NS + NS], start=True,
                        stop=True),
                        deps=[t_wT_all, t_be2,
                              (t_bw_prev if t_bw_prev is not None else t_acts_all)])
                    ty1 = p.op(p.ac, lambda a, dc=dc, ns_i=ns_i, bank=bank: a.activation(
                        yr[:, dc * HT + ns_i * NS:dc * HT + ns_i * NS + NS],
                        bank[:, :], AF.Identity,
                        bias=c_bias[:, BC_BS2 + dc:BC_BS2 + dc + 1]),
                        deps=[mm])
                    ty2 = p.op(p.ve, lambda v, dc=dc, ns_i=ns_i: v.tensor_tensor(
                        yr[:, dc * HT + ns_i * NS:dc * HT + ns_i * NS + NS],
                        yr[:, dc * HT + ns_i * NS:dc * HT + ns_i * NS + NS],
                        psG[:, :], ALU.add), deps=[ty1, mw])
                    t_yr[it] = ty2
                    t_bw_prev = ty2
                wrelease(pp2, ns2, t_yr[dc * 2 + 1])
            t_yr_all = t_yr[15]

            t_prev_exp_yr = None
            t_wb_war = None
            for e in range(E):
                t_wb = []
                for ns_i in range(2):
                    mb = p.op(p.pe, lambda t, e=e, ns_i=ns_i: t.matmul(
                        psG[:, :], c_sel[:, e * 128:(e + 1) * 128],
                        wTt[:, ns_i * NS:ns_i * NS + NS],
                        start=True, stop=True),
                        deps=[t_wT_all, TK_SEL,
                              (t_wb_war if t_wb_war is not None else t_yr_all)])
                    ev = p.op(p.ve, lambda v, ns_i=ns_i: v.tensor_copy(
                        wb[:, ns_i * NS:ns_i * NS + NS], psG[:, :]),
                        deps=[mb, (t_prev_exp_yr if t_prev_exp_yr is not None
                                   else t_yr_all)])
                    t_wb.append(ev)
                    t_wb_war = ev
                t_wb_all = t_wb[-1]

                t_eacts = {}
                wa = wg = None
                for sc in range(24):
                    if sc % 2 == 0:
                        wa = wload(we1f, sc * 128, (sc + 2) * 128, NF, 2, r0=e * D)
                        wg = wload(we1f, HR + sc * 128, HR + (sc + 2) * 128, NF, 2, r0=e * D)
                    offa, toka, ppa, nsa = wa
                    offg, tokg, ppg, nsg = wg
                    sub = sc % 2
                    for ns_i in range(2):
                        it = sc * 2 + ns_i
                        ba = psA if it % 2 == 0 else psC
                        bg = psB if it % 2 == 0 else psD
                        deps = [toka, t_xn2_all]
                        if it >= 2:
                            deps.append(t_eacts[it - 2])
                        elif e > 0:
                            deps.append(t_prev_exp_Bmm)
                        else:
                            deps.append(t_yr_all)
                        mma = None
                        for jc in range(NF):
                            mma = p.op(p.pe, lambda t, offa=offa, jc=jc, ba=ba,
                                       ns_i=ns_i, sub=sub: t.matmul(
                                ba[:, :],
                                wring[:, offa + jc * 256 + sub * 128:
                                       offa + jc * 256 + sub * 128 + 128],
                                xn2[:, jc * HT + ns_i * NS:
                                    jc * HT + ns_i * NS + NS],
                                start=(jc == 0), stop=(jc == NF - 1)),
                                deps=deps if jc == 0 else ())
                        mmg = None
                        for jc in range(NF):
                            mmg = p.op(p.pe, lambda t, offg=offg, jc=jc, bg=bg,
                                       ns_i=ns_i, sub=sub: t.matmul(
                                bg[:, :],
                                wring[:, offg + jc * 256 + sub * 128:
                                       offg + jc * 256 + sub * 128 + 128],
                                xn2[:, jc * HT + ns_i * NS:
                                    jc * HT + ns_i * NS + NS],
                                start=(jc == 0), stop=(jc == NF - 1)),
                                deps=[tokg] if jc == 0 else ())
                        bca = BC_BE1 + e * 48 + sc
                        bcg = BC_BE1 + e * 48 + 24 + sc
                        ta_ = p.op(p.ac, lambda a, ba=ba, it=it, bca=bca: a.activation(
                            aval[:, (it % 2) * HT:(it % 2) * HT + NS], ba[:, :],
                            AF.Identity, bias=c_bias[:, bca:bca + 1]), deps=[mma])
                        tg_ = p.op(p.ac, lambda a, bg=bg, it=it, bcg=bcg: a.activation(
                            gact[:, (it % 2) * HT:(it % 2) * HT + NS], bg[:, :],
                            AF.Gelu, bias=c_bias[:, bcg:bcg + 1]), deps=[mmg])
                        tm_ = p.op(p.ve, lambda v, sc=sc, ns_i=ns_i, it=it: v.tensor_tensor(
                            act[:, sc * HT + ns_i * NS:sc * HT + ns_i * NS + NS],
                            aval[:, (it % 2) * HT:(it % 2) * HT + NS],
                            gact[:, (it % 2) * HT:(it % 2) * HT + NS],
                            ALU.mult), deps=[ta_, tg_])
                        t_eacts[it] = tm_
                        if sub == 1 and ns_i == 1:
                            wrelease(ppa, nsa, mma)
                            wrelease(ppg, nsg, mmg)
                t_pa = t_eacts[47]

                t_yra = {}
                for dc in range(NF):
                    w2e = wload(we2f, dc * 128, (dc + 1) * 128, 24, 3, r0=e * HR)
                    off2, tok2, pp2, ns2 = w2e
                    for ns_i in range(2):
                        it = dc * 2 + ns_i
                        bank = psC if it % 2 == 0 else psD
                        deps = [tok2, t_pa, t_wb_all]
                        if it >= 2:
                            deps.append(t_yra[it - 2])
                        elif e > 0:
                            deps.append(t_prev_exp_B.get(it))
                        else:
                            deps.append(t_xn2_all)
                        mm = None
                        for sc in range(24):
                            mm = p.op(p.pe, lambda t, off2=off2, sc=sc,
                                      bank=bank, ns_i=ns_i: t.matmul(
                                bank[:, :],
                                wring[:, off2 + sc * 128:off2 + (sc + 1) * 128],
                                act[:, sc * HT + ns_i * NS:
                                    sc * HT + ns_i * NS + NS],
                                start=(sc == 0), stop=(sc == 23)),
                                deps=deps if sc == 0 else ())
                        tw = p.op(p.ve, lambda v, it=it, bank=bank, ns_i=ns_i: v.tensor_tensor(
                            tsc[:, (it % 2) * NS:(it % 2) * NS + NS],
                            bank[:, :], wb[:, ns_i * NS:ns_i * NS + NS],
                            ALU.mult), deps=[mm])
                        ta2 = p.op(p.ve, lambda v, dc=dc, ns_i=ns_i, it=it: v.tensor_tensor(
                            yr[:, dc * HT + ns_i * NS:dc * HT + ns_i * NS + NS],
                            yr[:, dc * HT + ns_i * NS:dc * HT + ns_i * NS + NS],
                            tsc[:, (it % 2) * NS:(it % 2) * NS + NS],
                            ALU.add),
                            deps=[tw, (t_yr[it] if e == 0 else None)])
                        t_yra[it] = ta2
                    wrelease(pp2, ns2, t_yra[dc * 2 + 1])
                t_prev_exp_B = t_yra
                t_prev_exp_Bmm = mm
                t_prev_exp_yr = t_yra[15]

            fin_toks = []
            prev_norm = t_prev_exp_yr
            for ns_i in range(2):
                t_rb4, t_bb4 = emit_ln(
                    lambda fc, ns_i=ns_i: yr[:, fc * HT + ns_i * NS:
                                             fc * HT + ns_i * NS + NS],
                    t_prev_exp_yr, (psA, psB, psC, psD), sqpp,
                    extra=prev_norm)
                for fc in range(NF):
                    sl = slice((fc % 2) * NS, (fc % 2) * NS + NS)
                    tm = p.op(p.ve, lambda v, fc=fc, ns_i=ns_i, sl=sl: v.tensor_tensor(
                        tsc[:, sl],
                        yr[:, fc * HT + ns_i * NS:fc * HT + ns_i * NS + NS],
                        psA[:, :], ALU.mult), deps=[t_rb4, t_prev_exp_yr])
                    tn = p.op(p.ve, lambda v, fc=fc, ns_i=ns_i, sl=sl: v.tensor_tensor(
                        tsc[:, sl], tsc[:, sl], psD[:, :], ALU.add),
                        deps=[tm, t_bb4])
                    to = p.op(p.ve, lambda v, fc=fc, ns_i=ns_i, sl=sl: v.tensor_tensor(
                        x1h[:, fc * HT + ns_i * NS:fc * HT + ns_i * NS + NS],
                        x1h[:, fc * HT + ns_i * NS:fc * HT + ns_i * NS + NS],
                        tsc[:, sl], ALU.add), deps=[tn])
                    fin_toks.append(to)
                prev_norm = fin_toks[-1]
            t_sto = p.dma(p.sy, p.st, lambda s, hsl=hsl: s.dma_start(
                out.rearrange("(fc p) t -> p fc t", p=128)[:, :, hsl],
                x1h.ap().rearrange("p (fc t) -> p fc t", fc=NF)),
                deps=fin_toks)
            st_x1.append(t_sto)

    if PHASE_LIMIT >= 2:
        for hf in range(2):
            _do_half(hf)

    for tkn in st_x1:
        p.wait(p.sy, tkn)

    p.finalize()
    sems = []
    for e in (p.pe, p.ve, p.ac, p.gp, p.sy, p.wdx, p.xd, p.st, *p.wds, *p.xds):
        cm = nc.semaphore(f"sem_{e.name}")
        e.sem = cm.__enter__()
        sems.append(cm)
    blk = nc.Block()
    block = blk.__enter__()

    @block.tensor
    def _(t):
        p.replay(p.pe, t)

    @block.vector
    def _(v):
        p.replay(p.ve, v)

    @block.scalar
    def _(a):
        p.replay(p.ac, a)

    @block.gpsimd
    def _(g):
        p.replay(p.gp, g)

    @block.sync
    def _(s):
        p.replay(p.sy, s)

    blk.__exit__(None, None, None)
    for cm in reversed(sems):
        cm.__exit__(None, None, None)
    while ctx:
        ctx.pop().__exit__(None, None, None)
    return nc


_CACHED = {}

_SEL0 = np.zeros((128, 128), dtype=np.float32)
_SEL0[0, :] = 1.0
_BDIAG = np.zeros((128, 128), dtype=np.float32)
_BDIAG[0:64, 0:64] = 1.0
_BDIAG[64:128, 64:128] = 1.0
_BDIAG = _BDIAG.astype(ml_dtypes.bfloat16)
_SEL = np.zeros((128, E * 128), dtype=np.float32)
for _e in range(E):
    _SEL[_e, _e * 128:(_e + 1) * 128] = 1.0


def _build_bias(bs1, bs2, be1):
    bias = np.zeros((128, NBCOL), dtype=np.float32)
    bias[:, BC_EPS] = EPS_LN
    bias[:, BC_BS1:BC_BS1 + 16] = bs1[:HS].reshape(16, 128).T
    bias[:, BC_BS1 + 16:BC_BS1 + 32] = bs1[HS:].reshape(16, 128).T
    bias[:, BC_BS2:BC_BS2 + 8] = bs2.reshape(8, 128).T
    for e in range(E):
        bias[:, BC_BE1 + e * 48:BC_BE1 + e * 48 + 24] = \
            be1[e, :HR].reshape(24, 128).T
        bias[:, BC_BE1 + e * 48 + 24:BC_BE1 + e * 48 + 48] = \
            be1[e, HR:].reshape(24, 128).T
    return bias


def kernel(L0_attn_x, L0_attn_Wq, L0_attn_Wk, L0_attn_Wv, L0_attn_Wo,
           L0_ffn_Ws1, L0_ffn_bs1, L0_ffn_Ws2, L0_ffn_bs2, L0_ffn_We1,
           L0_ffn_be1, L0_ffn_We2, L0_ffn_be2, L0_ffn_Wr, L0_ffn_br):
    if "nc" not in _CACHED:
        _CACHED["nc"] = build_program()
    nc = _CACHED["nc"]

    x = np.asarray(L0_attn_x, dtype=np.float32)
    shared = {
        "wqT": np.ascontiguousarray(np.asarray(L0_attn_Wq, np.float32).T),
        "wkT": np.ascontiguousarray(np.asarray(L0_attn_Wk, np.float32).T),
        "wvT": np.ascontiguousarray(np.asarray(L0_attn_Wv, np.float32).T),
        "woT": np.ascontiguousarray(np.asarray(L0_attn_Wo, np.float32).T),
        "ws1": np.ascontiguousarray(np.asarray(L0_ffn_Ws1, np.float32)),
        "ws2": np.ascontiguousarray(np.asarray(L0_ffn_Ws2, np.float32)),
        "we1": np.ascontiguousarray(np.asarray(L0_ffn_We1, np.float32)),
        "we2": np.ascontiguousarray(np.asarray(L0_ffn_We2, np.float32)),
        "wr": np.ascontiguousarray(np.asarray(L0_ffn_Wr, np.float32)),
        "be2": np.ascontiguousarray(np.asarray(L0_ffn_be2, np.float32)),
        "bias": _build_bias(np.asarray(L0_ffn_bs1, np.float32),
                            np.asarray(L0_ffn_bs2, np.float32),
                            np.asarray(L0_ffn_be1, np.float32)),
        "brb": np.ascontiguousarray(
            np.broadcast_to(np.asarray(L0_ffn_br, np.float32)[None, :],
                            (128, E))),
        "maskT": np.triu(np.ones((CH, CH), np.float32)).astype(ml_dtypes.bfloat16),
        "bdiag": _BDIAG,
        "idb": np.eye(128, dtype=np.float32).astype(ml_dtypes.bfloat16),
        "idf": np.eye(128, dtype=np.float32),
        "onesT": np.ones((128, 128), dtype=np.float32),
        "sel0": _SEL0,
        "sel": _SEL,
        "onebf": np.ones((128, 1), np.float32).astype(ml_dtypes.bfloat16),
    }
    in_maps = []
    for i in range(8):
        m = dict(shared)
        m["xT"] = np.ascontiguousarray(x[i].T)
        in_maps.append(m)

    res = run_bass_kernel_spmd(nc, in_maps, core_ids=list(range(8)))
    outs = [np.ascontiguousarray(r["out"].T) for r in res.results]
    return np.stack(outs, axis=0)
